# version 13
# baseline (speedup 1.0000x reference)
from contextlib import ExitStack

import numpy as np
import ml_dtypes
import concourse.bass as bass
import concourse.mybir as mybir
from concourse.bass_utils import run_bass_kernel_spmd

F32 = mybir.dt.float32
BF16 = mybir.dt.bfloat16
AF = mybir.ActivationFunctionType
ALU = mybir.AluOpType
AX = mybir.AxisListType
BF = ml_dtypes.bfloat16

D = 1024
H = 16
NCORES = 8
OWN = 2048
FFA = 2816
FFH = 4096
QT_N = OWN // 128


def _offset_runs(positions):
    runs = []
    i = 0
    pos = list(positions)
    n = len(pos)
    while i < n:
        if i + 1 < n:
            step = pos[i + 1] - pos[i]
            j = i + 1
            while j + 1 < n and pos[j + 1] - pos[j] == step:
                j += 1
            runs.append((i, step, j - i + 1))
            i = j + 1
        else:
            runs.append((i, 1, 1))
            i += 1
    return runs


class Ctr:
    def __init__(self, h):
        self.h = h
        self.n = 0


class Prog:
    def __init__(self):
        self.q = {"tensor": [], "vector": [], "scalar": [], "gpsimd": [], "sync": []}

    def emit(self, eng, fn, incs=(), waits=()):
        lst = self.q[eng]
        for (c, v) in waits:
            if v > 0:
                lst.append(lambda e, _h=c.h, _v=v: e.wait_ge(_h, _v))
        ivals = []
        for (c, k) in incs:
            c.n += k
            ivals.append((c.h, k))

        def run(e, _fn=fn, _iv=tuple(ivals)):
            inst = _fn(e)
            for (_h, _k) in _iv:
                inst = inst.then_inc(_h, _k)

        lst.append(run)


def _finish(nc, P, dma_sems):
    for c_ in dma_sems.values():
        if c_.n > 0:
            P.emit("sync", lambda s, _h=c_.h, _v=c_.n: s.wait_ge(_h, _v))
    with nc.Block() as blk:
        blk.tensor(lambda t: [f(t) for f in P.q["tensor"]])
        blk.vector(lambda v: [f(v) for f in P.q["vector"]])
        blk.scalar(lambda s: [f(s) for f in P.q["scalar"]])
        blk.gpsimd(lambda g: [f(g) for f in P.q["gpsimd"]])
        blk.sync(lambda s: [f(s) for f in P.q["sync"]])


def build_nc(offsets, tap=None, upto=99, p2sub=99):
    offs = sorted(int(o) for o in np.asarray(offsets).tolist())
    OMAX = max(offs)
    HALO = ((OMAX + 127) // 128) * 128
    TOT = OWN + HALO
    NB = TOT // 512
    NT = TOT // 128
    HT = HALO // 128

    Jset = set()
    for o in offs:
        Jset.add((-o) // 128)
        Jset.add((127 - o) // 128)
    J = sorted(Jset, reverse=True)
    NJ = len(J)
    assert min(J) >= -HT and NJ % 4 == 0
    GL = ((OMAX + 256 + 511) // 512) * 512

    nc = bass.Bass("TRN2", target_bir_lowering=False)

    def din(name, shape, dt=F32):
        return nc.dram_tensor(name, shape, dt, kind="ExternalInput")

    xT = din("xT", [D, TOT])
    wqkv = din("wqkv", [D, 3 * D])
    wgate = din("wgate", [D, D])
    wout = din("wout", [D, D])
    wfg = din("wfg", [D, FFA])
    wfu = din("wfu", [D, FFA])
    wfd = din("wfd", [FFA, D])
    w1T = din("w1T", [D, FFH])
    w2T = din("w2T", [D, FFH])
    h1T = din("h1T", [FFH, FFH])
    h2T = din("h2T", [FFH, FFH])
    w3T = din("w3T", [FFH, D])
    g1v = din("g1v", [D])
    g2v = din("g2v", [D])
    bgv = din("bgv", [D])
    s1v = din("s1v", [FFH])
    s2v = din("s2v", [FFH])
    sh1v = din("sh1v", [FFH])
    sh2v = din("sh2v", [FFH])
    s3v = din("s3v", [D])
    posb = din("posb", [len(offs), H])
    tbase = din("tbase", [128, 1])
    c_iota = din("c_iota", [128, 1])
    c_onesf = din("c_onesf", [1, 128])
    c_onesb = din("c_onesb", [128, 128], BF16)
    c_onescb = din("c_onescb", [128, 1], BF16)
    c_id = din("c_id", [128, 128], BF16)
    c_cv = din("c_cv", [1, 4])

    out_t = nc.dram_tensor("out", [D, OWN], F32, kind="ExternalOutput")
    tap_t = None
    if tap is not None:
        tap_t = nc.dram_tensor("tapout", list(tap[1]), tap[2], kind="ExternalOutput")

    d_xn = nc.dram_tensor("d_xn", [8, 128, TOT], BF16, kind="Internal")
    d_q = nc.dram_tensor("d_q", [8, 128, OWN], BF16, kind="Internal")
    d_gate = nc.dram_tensor("d_gate", [QT_N, 128, D], BF16, kind="Internal")
    d_gated = nc.dram_tensor("d_gated", [QT_N, 128, D], BF16, kind="Internal")
    d_r1 = nc.dram_tensor("d_r1", [8, 128, OWN], F32, kind="Internal")
    d_r2 = nc.dram_tensor("d_r2", [8, 128, OWN], F32, kind="Internal")
    d_gtab = nc.dram_tensor("d_gtab", [H, GL], BF16, kind="Internal")
    d_grep = nc.dram_tensor("d_grep", [H, 128, GL], BF16, kind="Internal")
    d_nb = nc.dram_tensor("d_nb", [2, H, 128], BF16, kind="Internal")
    d_negb = nc.dram_tensor("d_negb", [2], F32, kind="Internal")

    P = Prog()
    ctx = ExitStack()

    def sem(name):
        return Ctr(ctx.enter_context(nc.semaphore(name)))

    tM = sem("tM")
    vC = sem("vC")
    sC = sem("sC")
    _dma_sems = {}

    def DS(name):
        if name not in _dma_sems:
            _dma_sems[name] = sem(name)
        return _dma_sems[name]

    def sb(name, shape, dt, stack=None, side=None):
        if side is None:
            return (stack or ctx).enter_context(nc.sbuf_tensor(name, shape, dt))
        return (stack or ctx).enter_context(nc.sbuf_tensor(name, shape, dt, side=side))

    def ps(name, shape, dt=F32, stack=None):
        return (stack or ctx).enter_context(nc.psum_tensor(name, shape, dt))

    def T(fn, incs=(), waits=()):
        P.emit("tensor", fn, incs, waits)

    def V(fn, waits=()):
        P.emit("vector", fn, incs=[(vC, 1)], waits=[(vC, vC.n)] + list(waits))

    def SE(fn, waits=()):
        P.emit("scalar", fn, incs=[(sC, 1)], waits=[(sC, sC.n)] + list(waits))

    def DW(sm, fn, waits=()):
        P.emit("gpsimd", fn, incs=[(sm, 16)], waits=waits)

    def DA(sm, fn, waits=()):
        P.emit("sync", fn, incs=[(sm, 16)], waits=waits)

    def DSC(sm, fn, waits=()):
        P.emit("scalar", fn, incs=[(sm, 16)], waits=waits)

    def barrier():
        vals = [(tM, tM.n), (vC, vC.n), (sC, sC.n)]
        vals += [(c, c.n) for c in _dma_sems.values()]
        for eng in ("tensor", "vector", "scalar", "gpsimd", "sync"):
            for (c, v) in vals:
                if v > 0:
                    P.emit(eng, lambda e, _h=c.h, _v=v: e.wait_ge(_h, _v))

    s_onesf = sb("s_onesf", [1, 128], F32)
    s_onesb = sb("s_onesb", [128, 128], BF16)
    s_onescb = sb("s_onescb", [128, 1], BF16)
    s_id = sb("s_id", [128, 128], BF16)
    s_iota = sb("s_iota", [128, 1], F32)
    s_tb = sb("s_tb", [128, 1], F32)
    s_tvec = sb("s_tvec", [128, 1], F32)
    s_g1 = sb("s_g1", [128, 8], F32)
    s_g2 = sb("s_g2", [128, 8], F32)
    s_s1 = sb("s_s1", [128, 32], F32)
    s_s2 = sb("s_s2", [128, 32], F32)
    s_sh1 = sb("s_sh1", [128, 32], F32)
    s_sh2 = sb("s_sh2", [128, 32], F32)
    s_s3 = sb("s_s3", [128, 8], F32)
    s_bgbc = sb("s_bgbc", [128, D], F32)
    s_cv = sb("s_cv", [1, 4], F32)

    qC = DS("qC")
    DA(qC, lambda s: s.dma_start(s_onesf[:, :], c_onesf.ap()))
    DA(qC, lambda s: s.dma_start(s_onesb[:, :], c_onesb.ap()))
    DA(qC, lambda s: s.dma_start(s_onescb[:, :], c_onescb.ap()))
    DA(qC, lambda s: s.dma_start(s_id[:, :], c_id.ap()))
    DA(qC, lambda s: s.dma_start(s_iota[:, :], c_iota.ap()))
    DA(qC, lambda s: s.dma_start(s_tb[:, :], tbase.ap()))
    for c in range(8):
        DA(qC, lambda s, c=c: s.dma_start(s_g1[:, c:c + 1], g1v.ap()[c * 128:(c + 1) * 128].unsqueeze(1)))
    DA(qC, lambda s: s.dma_start(s_cv[:, :], c_cv.ap()))
    DA(qC, lambda s: s.dma_start(s_bgbc[:, :], bass.AP(bgv, 0, [[0, 128], [1, D]])))
    consts_q = qC.n

    V(lambda v: v.tensor_add(s_tvec[:, :], s_iota[:, :], s_tb[:, :]),
      waits=[(qC, consts_q)])

    st0 = ExitStack()
    s_xa = sb("s_xa", [128, 8 * TOT], BF16, st0, side="right")
    xa3 = s_xa[:, :].rearrange("p (c t) -> p c t", c=8)
    qG = DS("qG")
    stg = ExitStack()
    s_gpb = sb("s_gpb", [H, GL], F32, stg, side="right")
    s_gme = sb("s_gme", [H, GL], BF16, stg, side="right")
    V(lambda v: v.memset(s_gpb[:, :], -1.0e30))
    gpb_ms = vC.n
    for (i0, step, cnt) in _offset_runs([o + 127 for o in offs]):
        p0 = offs[i0] + 127

        def scat(s, i0=i0, step=step, cnt=cnt, p0=p0):
            src = posb.ap()[i0:i0 + cnt, :].transpose([1, 0])
            dst = s_gpb[:, p0:p0 + step * cnt:step] if cnt > 1 else s_gpb[:, p0:p0 + 1]
            return s.dma_start(dst, src, allow_slow_non_contiguous=True)
        DA(qG, scat, waits=[(vC, gpb_ms), (qC, consts_q)])
    scat_q = qG.n
    SE(lambda s: s.activation(s_gme[:, :], s_gpb[:, :], AF.Exp),
       waits=[(qG, scat_q)])
    gme_ms = sC.n
    DA(qG, lambda s: s.dma_start(d_gtab.ap(), s_gme[:, :]), waits=[(sC, gme_ms)])
    gtab_q = qG.n
    for h in range(H):
        DA(qG, lambda s, h=h: s.dma_start(
            d_grep.ap()[h], bass.AP(d_gtab, h * GL, [[0, 128], [1, GL]])),
           waits=[(qG, gtab_q)])
    grep_q = qG.n
    g_read_done = (gme_ms, gtab_q)
    stg.close()

    st0b = ExitStack()
    s_row = sb("s_row", [1, 512 * 4], F32, st0b, side="right")
    s_x32 = [sb("s_x320", [128, 8 * 512], F32, st0b, side="right"),
             sb("s_x321", [128, 8 * 512], F32, st0b, side="right")]
    s_xb = [sb("s_xb0", [128, 8 * 512], BF16, st0b, side="right"),
            sb("s_xb1", [128, 8 * 512], BF16, st0b, side="right")]
    s_xnb = [sb("s_xnb0", [128, 8 * 512], BF16, st0b, side="right"),
             sb("s_xnb1", [128, 8 * 512], BF16, st0b, side="right")]
    p_s1 = [ps("p_s1a", [1, 512], F32, st0b), ps("p_s1b", [1, 512], F32, st0b)]
    p_s2 = [ps("p_s2a", [1, 512], F32, st0b), ps("p_s2b", [1, 512], F32, st0b)]
    p_b1 = ps("p_b1", [128, 512], F32, st0b)
    p_b2 = ps("p_b2", [128, 512], F32, st0b)
    p_b3 = ps("p_b3", [128, 512], F32, st0b)

    xTv = xT.ap().rearrange("(c p) t -> p c t", p=128)
    xn_w = d_xn.ap().rearrange("c p t -> p c t")

    def row(i):
        return s_row[:, i * 512:(i + 1) * 512]

    qX0 = [DS("qX00"), DS("qX01")]
    wXB = [DS("wXB0"), DS("wXB1")]
    qXN = [DS("qXN0"), DS("qXN1")]
    p0 = {
        "apl_v": [0, 0], "sq_v": [0, 0], "stat_tm": [0, 0], "statevac_v": [0, 0],
        "xq": [0, 0], "wq": [0, 0], "xnst": [0, 0], "bc_ms": 0, "stat_ms": [0, 0],
    }

    def p0_load(blk):
        p = blk & 1
        cslice = slice(blk * 512, (blk + 1) * 512)
        gw = [(sC, g_read_done[0]), (qG, g_read_done[1])] if blk == 0 else []
        DSC(qX0[p], lambda s, cslice=cslice, p=p: s.dma_start(
            s_x32[p][:, :].rearrange("p (c t) -> p c t", c=8), xTv[:, :, cslice]),
            waits=[(vC, p0["apl_v"][p])] + gw)
        p0["xq"][p] = qX0[p].n
        DW(wXB[p], lambda g, cslice=cslice, p=p: g.dma_start(
            s_xb[p][:, :].rearrange("p (c t) -> p c t", c=8), xTv[:, :, cslice]),
           waits=[(vC, p0["sq_v"][p]), (tM, p0["stat_tm"][p])] + gw)
        p0["wq"][p] = wXB[p].n

    def p0_stats(blk):
        p = blk & 1
        xb3 = s_xb[p][:, :].rearrange("p (c t) -> p c t", c=8)
        sq3 = s_xnb[p][:, :].rearrange("p (c t) -> p c t", c=8)
        V(lambda v, p=p: v.tensor_mul(s_xnb[p][:, :], s_xb[p][:, :], s_xb[p][:, :]),
          waits=[(wXB[p], p0["wq"][p]), (qXN[p], p0["xnst"][p])])
        sq_ms = vC.n
        p0["sq_v"][p] = sq_ms
        for c in range(8):
            T(lambda t, c=c, p=p, xb3=xb3: t.matmul(p_s1[p][:, :], s_onescb[:, :], xb3[:, c, :],
                                                    start=(c == 0), stop=(c == 7)),
              waits=[(wXB[p], p0["wq"][p]), (vC, p0["statevac_v"][p])] if c == 0 else ())
        for c in range(8):
            T(lambda t, c=c, p=p, sq3=sq3: t.matmul(p_s2[p][:, :], s_onescb[:, :], sq3[:, c, :],
                                                    start=(c == 0), stop=(c == 7)),
              incs=[(tM, 1)] if c == 7 else (),
              waits=[(vC, sq_ms)] if c == 0 else ())
        p0["stat_ms"][p] = tM.n
        p0["stat_tm"][p] = tM.n

    def p0_rows(blk):
        p = blk & 1
        V(lambda v, p=p: v.tensor_copy(row(0), p_s1[p][:, :]),
          waits=[(tM, p0["stat_ms"][p]), (tM, p0["bc_ms"])])
        V(lambda v, p=p: v.tensor_copy(row(1), p_s2[p][:, :]))
        p0["statevac_v"][p] = vC.n
        V(lambda v: v.tensor_scalar_mul(row(0), row(0), 1.0 / D))
        V(lambda v: v.tensor_mul(row(2), row(0), row(0)))
        V(lambda v: v.tensor_scalar_mul(row(1), row(1), 1.0 / D))
        V(lambda v: v.tensor_sub(row(1), row(1), row(2)))
        vvar = vC.n
        SE(lambda s: s.activation(row(2), row(1), AF.Ln, bias=s_cv[0:1, 0:1]), waits=[(vC, vvar)])
        SE(lambda s: s.activation(row(2), row(2), AF.Exp, scale=s_cv[0:1, 2:3]))
        se_rstd = sC.n
        V(lambda v: v.tensor_scalar_add(row(3), row(1), 1.0e-5), waits=[(sC, se_rstd)])
        V(lambda v: v.reciprocal(row(3), row(3)))
        V(lambda v: v.tensor_scalar(row(3), row(3), -1.0e-5, 1.0 + 1.0e-6, ALU.mult, ALU.add))
        vq_ = vC.n
        SE(lambda s: s.activation(row(3), row(3), AF.Ln), waits=[(vC, vq_)])
        SE(lambda s: s.activation(row(3), row(3), AF.Exp, scale=s_cv[0:1, 2:3]))
        se_rr = sC.n
        V(lambda v: v.tensor_scalar_mul(row(0), row(0), -1.0))
        V(lambda v: v.tensor_mul(row(3), row(3), row(2)), waits=[(sC, se_rr)])
        vsm = vC.n
        T(lambda t: t.matmul(p_b1[:, :], s_onesf[:, :], row(0), start=True, stop=True),
          waits=[(vC, vsm), (sC, se_rr)])
        T(lambda t: t.matmul(p_b2[:, :], s_onesf[:, :], row(2), start=True, stop=True))
        T(lambda t: t.matmul(p_b3[:, :], s_onesf[:, :], row(3), start=True, stop=True),
          incs=[(tM, 1)])
        p0["bc_ms"] = tM.n

    def p0_apply(blk):
        p = blk & 1
        cslice = slice(blk * 512, (blk + 1) * 512)
        x323 = s_x32[p][:, :].rearrange("p (c t) -> p c t", c=8)
        xnb3 = s_xnb[p][:, :].rearrange("p (c t) -> p c t", c=8)
        bc = p0["bc_ms"]
        for c in range(8):
            V(lambda v, c=c, x323=x323: v.tensor_tensor(
                x323[:, c, :], x323[:, c, :], p_b1[:, :], ALU.add),
              waits=[(tM, bc), (qX0[p], p0["xq"][p])] if c == 0 else ())
            V(lambda v, c=c, x323=x323, xnb3=xnb3: v.tensor_tensor(
                xnb3[:, c, :], x323[:, c, :], p_b2[:, :], ALU.mult))
            V(lambda v, c=c, cslice=cslice, x323=x323: v.scalar_tensor_tensor(
                xa3[:, c, cslice], x323[:, c, :], s_g1[:, c:c + 1], p_b3[:, :],
                ALU.mult, ALU.mult))
        apl_ms = vC.n
        p0["apl_v"][p] = apl_ms
        DW(qXN[p], lambda s, cslice=cslice, p=p: s.dma_start(
            xn_w[:, :, cslice], s_xnb[p][:, :].rearrange("p (c t) -> p c t", c=8)),
           waits=[(vC, apl_ms)])
        p0["xnst"][p] = qXN[p].n

    p0_load(0)
    p0_stats(0)
    for blk in range(NB):
        if blk + 1 < NB:
            p0_load(blk + 1)
        p0_rows(blk)
        if blk + 1 < NB:
            p0_stats(blk + 1)
        p0_apply(blk)

    barrier()
    st0b.close()
    if upto == 0:
        _finish(nc, P, _dma_sems)
        st0.close()
        ctx.close()
        return nc

    st_kv = ExitStack()
    s_K = sb("s_K", [128, 8 * TOT], BF16, st_kv)
    K3 = s_K[:, :].rearrange("p (c t) -> p c t", c=8)
    s_V = sb("s_V", [128, NT * H * 65], BF16, st_kv)
    V4 = s_V[:, :].rearrange("p (n h e) -> p n h e", n=NT, h=H)
    st1b = ExitStack()
    s_w8 = sb("s_w8", [128, 8 * 512], BF16, st1b, side="right")
    w83 = s_w8[:, :].rearrange("p (k o) -> p k o", k=8)
    s_w832 = sb("s_w832", [128, 8 * 512], F32, st1b, side="right")
    w8323 = s_w832[:, :].rearrange("p (k o) -> p k o", k=8)
    s_evq = sb("s_evq", [128, 1024], BF16, st1b, side="right")
    s_gt32 = sb("s_gt32", [128, 512], F32, st1b, side="right")
    s_vc = sb("s_vc", [128, 1], F32, st1b, side="right")
    pg = [ps("p_gm0", [128, 1024], F32, st1b), ps("p_gm1", [128, 1024], F32, st1b)]
    wSL = DS("wSL")
    qWS = DS("qWS")
    qES = DS("qES")
    p1w = {"cast_prev": 0, "mms_done": 0}

    def p1_slab(src_slab):
        DA(qWS, lambda s, src_slab=src_slab: s.dma_start(w8323[:, :, :], src_slab),
           waits=[(vC, p1w["cast_prev"])])
        wq = qWS.n
        V(lambda v: v.tensor_copy(s_w8[:, :], s_w832[:, :]),
          waits=[(qWS, wq), (tM, p1w["mms_done"])])
        p1w["cast_prev"] = vC.n
        return vC.n
    pg_free = [0, 0]
    pi = 0

    wqkv_v = wqkv.ap().rearrange("(k p) o -> p k o", p=128)
    wgate_v = wgate.ap().rearrange("(k p) o -> p k o", p=128)

    for (base_col, is_q) in ((D, False), (0, True)):
        for wh in range(2):
            cast_v = p1_slab(wqkv_v[:, :, base_col + wh * 512: base_col + (wh + 1) * 512])
            grps = [(3, 2), (5, 2)] if is_q else [(0, 2), (2, 2), (4, 2), (6, 1)]
            for oc4 in range(4):
                oc = wh * 4 + oc4
                for (b0, gn) in grps:
                    p = pg[pi]
                    fw = [(vC, pg_free[pi])] + ([(vC, cast_v)] if (oc4 == 0 and b0 == grps[0][0]) else [])
                    for kc in range(8):
                        for b in range(gn):
                            T(lambda t, oc4=oc4, kc=kc, b0=b0, b=b, p=p: t.matmul(
                                p[:, b * 512:(b + 1) * 512],
                                w83[:, kc, oc4 * 128:(oc4 + 1) * 128],
                                xa3[:, kc, (b0 + b) * 512:(b0 + b + 1) * 512],
                                start=(kc == 0), stop=(kc == 7)),
                              incs=[(tM, 1)] if (kc == 7 and b == gn - 1) else (),
                              waits=fw if (kc == 0 and b == 0) else ())
                    mm_ms = tM.n
                    if is_q:
                        V(lambda v, p=p: v.tensor_scalar_mul(s_evq[:, :], p[:, :], 0.125),
                          waits=[(tM, mm_ms), (qES, qES.n)])
                        ev_ms = vC.n
                        DA(qES, lambda s, oc=oc, b0=b0: s.dma_start(
                            d_q.ap()[oc][:, (b0 - 3) * 512:(b0 - 1) * 512], s_evq[:, :]),
                           waits=[(vC, ev_ms)])
                    else:
                        V(lambda v, oc=oc, b0=b0, gn=gn, p=p: v.tensor_copy(
                            K3[:, oc, b0 * 512:(b0 + gn) * 512], p[:, 0:gn * 512]),
                          waits=[(tM, mm_ms)])
                        ev_ms = vC.n
                    pg_free[pi] = ev_ms
                    pi ^= 1
            p1w["mms_done"] = tM.n

    for wh in range(2):
        cast_v = p1_slab(wgate_v[:, :, wh * 512:(wh + 1) * 512])
        for tt in range(QT_N):
            p = pg[pi]
            fw = [(vC, pg_free[pi])] + ([(vC, cast_v)] if tt == 0 else [])
            for kc in range(8):
                T(lambda t, tt=tt, kc=kc, p=p: t.matmul(
                    p[:, 0:512],
                    xa3[:, kc, HALO + tt * 128: HALO + (tt + 1) * 128],
                    w83[:, kc, :],
                    start=(kc == 0), stop=(kc == 7)),
                  incs=[(tM, 1)] if kc == 7 else (),
                  waits=fw if kc == 0 else ())
            mm_ms = tM.n
            V(lambda v, p=p, wh=wh: v.tensor_tensor(
                s_gt32[:, :], p[:, 0:512], s_bgbc[:, wh * 512:(wh + 1) * 512], ALU.add),
              waits=[(tM, mm_ms), (sC, sC.n)])
            pg_free[pi] = vC.n
            pi ^= 1
            vv = vC.n
            SE(lambda s: s.activation(s_evq[:, 0:512], s_gt32[:, :], AF.Sigmoid),
               waits=[(vC, vv), (qES, qES.n)])
            sv = sC.n
            DA(qES, lambda s, tt=tt, wh=wh: s.dma_start(
                d_gate.ap()[tt][:, wh * 512:(wh + 1) * 512], s_evq[:, 0:512]),
               waits=[(sC, sv)])
        p1w["mms_done"] = tM.n

    for wh in range(2):
        cast_v = p1_slab(wqkv_v[:, :, 2 * D + wh * 512: 2 * D + (wh + 1) * 512])
        for tt in range(NT):
            p = pg[pi]
            fw = [(vC, pg_free[pi])] + ([(vC, cast_v)] if tt == 0 else [])
            for kc in range(8):
                T(lambda t, tt=tt, kc=kc, p=p: t.matmul(
                    p[:, 0:512],
                    xa3[:, kc, tt * 128:(tt + 1) * 128],
                    w83[:, kc, :],
                    start=(kc == 0), stop=(kc == 7)),
                  incs=[(tM, 1)] if kc == 7 else (),
                  waits=fw if kc == 0 else ())
            mm_ms = tM.n
            V(lambda v, tt=tt, p=p, wh=wh: v.tensor_copy(
                V4[:, tt, wh * 8:(wh + 1) * 8, 0:64],
                p[:, 0:512].rearrange("p (h d) -> p h d", h=8)),
              waits=[(tM, mm_ms)])
            V(lambda v, tt=tt: v.tensor_single_scalar(
                s_vc[:, :], s_tvec[:, :], float(HALO - tt * 128), ALU.is_ge))
            V(lambda v, tt=tt, wh=wh: v.tensor_copy(
                V4[:, tt, wh * 8:(wh + 1) * 8, 64:65],
                s_vc[:, :].unsqueeze(1).broadcast_to((128, 8, 1))))
            pg_free[pi] = vC.n
            pi ^= 1

    barrier()
    st1b.close()
    st0.close()
    if upto == 1:
        _finish(nc, P, _dma_sems)
        st_kv.close()
        ctx.close()
        return nc

    qMT = DS("qMT")
    qQT = [DS("qQT0"), DS("qQT1")]
    qGT = [DS("qGT0"), DS("qGT1")]
    qGD = [DS("qGD0"), DS("qGD1")]
    qNB = [DS("qNB0"), DS("qNB1")]
    st2 = ExitStack()
    s_MT = sb("s_MT", [128, H * NJ * 128], BF16, st2)
    MT3 = s_MT[:, :].rearrange("p (h x) -> p h x", h=H)
    MT4 = s_MT[:, :].rearrange("p (h j q) -> p h j q", h=H, j=NJ)
    _offs_mt = [127 - 128 * J[jj] for jj in range(NJ)]
    for h in range(H):
        for (i0, stp, cnt_) in _offset_runs(_offs_mt):
            def mdma(s, h=h, i0=i0, stp=stp, cnt_=cnt_):
                src = bass.AP(d_grep, h * 128 * GL + _offs_mt[i0],
                              [[GL - 1, 128], [stp, cnt_], [1, 128]])
                return s.dma_start(MT4[:, h, i0:i0 + cnt_, :], src)
            DA(qMT, mdma, waits=[(qG, grep_q)] if h == 0 else ())
    mt_q = qMT.n

    s_qt = [sb("s_qt0", [128, 8 * 128], BF16, st2), sb("s_qt1", [128, 8 * 128], BF16, st2)]
    s_gtt = [sb("s_gtt0", [128, D], BF16, st2), sb("s_gtt1", [128, D], BF16, st2)]
    s_gd = [sb("s_gd0", [128, D], BF16, st2), sb("s_gd1", [128, D], BF16, st2)]
    s_b = [sb("s_b0", [128, 16], F32, st2), sb("s_b1", [128, 16], F32, st2)]
    s_bq = sb("s_bq", [128, 1], F32, st2)
    s_nbm = sb("s_nbm", [1, 1], F32, st2)
    s_negB = [sb("s_negB0", [128, 1], F32, st2), sb("s_negB1", [128, 1], F32, st2)]
    s_E = [sb("s_E0", [128, NJ * 128], BF16, st2), sb("s_E1", [128, NJ * 128], BF16, st2)]
    s_at = [sb("s_at0", [128, NJ * 128], BF16, st2), sb("s_at1", [128, NJ * 128], BF16, st2)]
    s_rd = sb("s_rd", [128, 1], F32, st2)
    p_A = [ps("p_A0", [128, 256], F32, st2), ps("p_A1", [128, 256], F32, st2)]
    p_B = [ps("p_B0", [128, NJ * 128], F32, st2), ps("p_B1", [128, NJ * 128], F32, st2)]
    p_av = [ps("p_av0", [128, 65], F32, st2), ps("p_av1", [128, 65], F32, st2)]
    s_nbp = sb("s_nbp", [128, 128], BF16, st2)
    s_nbTp = sb("s_nbTp", [128, 128], BF16, st2)

    bstride = 1 - min(J)
    st_p2 = {
        "pB_free": [0, 0],
        "mask_done": [0, 0],
        "av_done_t": [0, 0],
        "at_ms": {},
        "pb_ms": {},
        "exp_ms": {},
        "qld": [0, 0],
        "gld": [0, 0],
        "negb_q": [0, 0],
        "gd_store": [0, 0],
        "pA_guard": 0,
        "av_free": [0, 0],
        "exp_done": [0, 0],
    }

    def p2_tile_loads(qt):
        bi = qt & 1
        DA(qQT[bi], lambda s, qt=qt, bi=bi: s.dma_start(
            s_qt[bi][:, :].rearrange("p (c t) -> p c t", c=8),
            d_q.ap().rearrange("c p t -> p c t")[:, :, qt * 128:(qt + 1) * 128]),
           waits=[(tM, tM.n)])
        st_p2["qld"][bi] = qQT[bi].n
        DA(qGT[bi], lambda s, qt=qt, bi=bi: s.dma_start(s_gtt[bi][:, :], d_gate.ap()[qt]),
           waits=[(vC, vC.n)])
        st_p2["gld"][bi] = qGT[bi].n

    def p2_passA_grp(qt, grp):
        bi = qt & 1
        q3 = s_qt[bi][:, :].rearrange("p (c t) -> p c t", c=8)
        t0col = HALO + qt * 128
        start_col = t0col + 128 - bstride * 128
        for hh in range(4):
            h = grp * 4 + hh
            g, r0 = h // 2, (h % 2) * 64
            par2 = h % 2
            fw = []
            if hh == 0:
                fw = [(vC, st_p2["pA_guard"]), (qQT[bi], st_p2["qld"][bi])]
            T(lambda t, hh=hh, g=g, r0=r0, par2=par2, start_col=start_col, q3=q3: t.matmul(
                p_A[par2][:, (hh // 2) * 128:(hh // 2 + 1) * 128],
                q3[r0:r0 + 64, g, :],
                K3[r0:r0 + 64, g, start_col:start_col + bstride * 128:bstride],
                start=(hh < 2), stop=(hh >= 2),
                tile_position=(r0, 0)),
              incs=[(tM, 1)] if hh == 3 else (),
              waits=fw if hh == 0 else ())
        amss = tM.n
        for par2 in range(2):
            V(lambda v, qt=qt, bi=bi, grp=grp, par2=par2: v.tensor_reduce(
                s_b[bi][:, grp * 4 + par2: (grp + 1) * 4: 2],
                p_A[par2][:, :].rearrange("p (h k) -> p h k", h=2),
                axis=AX.X, op=ALU.max),
              waits=[(tM, amss)] if par2 == 0 else ())
        st_p2["pA_guard"] = vC.n

    def p2_passA_tail(qt):
        bi = qt & 1
        V(lambda v, bi=bi: v.tensor_reduce(s_bq[:, :], s_b[bi][:, :], axis=AX.X, op=ALU.max),
          waits=[(qNB[bi ^ 1], qNB[bi ^ 1].n), (qNB[bi], qNB[bi].n)])
        V(lambda v: v.tensor_scalar_mul(s_nbp[:, 0:1], s_bq[:, :], -1.0))
        nb_ms = vC.n
        DA(qNB[bi], lambda s: s.dma_start(s_nbTp[:, :], s_nbp[:, :], transpose=True),
           waits=[(vC, nb_ms)])
        nbq = qNB[bi].n
        V(lambda v, nbq=nbq: v.tensor_reduce(
            s_nbm[:, :], s_nbTp[0:1, :], axis=AX.X, op=ALU.min),
          waits=[(qNB[bi], nbq)])
        nbm_v = vC.n
        DA(qNB[bi], lambda s, bi=bi: s.dma_start(d_negb.ap()[bi:bi + 1].unsqueeze(0), s_nbm[:, :]),
           waits=[(vC, nbm_v)])
        DA(qNB[bi], lambda s, bi=bi: s.dma_start(
            s_negB[bi][:, :], bass.AP(d_negb, bi, [[0, 128], [1, 1]])),
           waits=[(qNB[bi], qNB[bi].n), (sC, sC.n)])
        st_p2["negb_q"][bi] = qNB[bi].n

    HW2 = (NJ * 128) // 2

    def p2_passB_mm(qt, h):
        bi = qt & 1
        par = h & 1
        g, r0 = h // 2, (h % 2) * 64
        pB = p_B[par]
        q3 = s_qt[bi][:, :].rearrange("p (c t) -> p c t", c=8)
        fw = [(sC, st_p2["exp_done"][par])] + \
             ([(qQT[bi], st_p2["qld"][bi])] if h == 0 else [])
        for jj in range(NJ):
            ktile = HT + qt + J[jj]
            T(lambda t, jj=jj, ktile=ktile, g=g, r0=r0, pB=pB, q3=q3: t.matmul(
                pB[:, jj * 128:(jj + 1) * 128],
                K3[r0:r0 + 64, g, ktile * 128:(ktile + 1) * 128],
                q3[r0:r0 + 64, g, :],
                start=(jj % (NJ // 2) == 0), stop=(jj % 4 == 3),
                tile_position=(r0, 0)),
              incs=[(tM, 1)] if jj == NJ - 1 else (),
              waits=fw if jj == 0 else ())
        st_p2["pb_ms"][h] = tM.n
        eh = []
        for hx in range(2):
            SE(lambda s, pB=pB, par=par, hx=hx, bi=bi: s.activation(
                s_E[par][:, hx * HW2:(hx + 1) * HW2], pB[:, hx * HW2:(hx + 1) * HW2], AF.Exp,
                bias=s_negB[bi][:, :]),
               waits=[(tM, st_p2["pb_ms"][h]), (qNB[bi], st_p2["negb_q"][bi])] +
                     ([(vC, st_p2["mask_done"][par])] if hx == 0 else []))
            eh.append(sC.n)
        st_p2["exp_done"][par] = eh[1]
        st_p2["exp_ms"][h] = tuple(eh)

    def p2_mask(qt, h):
        par = h & 1
        (e1, e2) = st_p2["exp_ms"][h]
        at_half = []
        for hx in range(2):
            V(lambda v, h=h, par=par, hx=hx: v.tensor_tensor(
                s_at[par][:, hx * HW2:(hx + 1) * HW2],
                s_E[par][:, hx * HW2:(hx + 1) * HW2],
                MT3[:, h, hx * HW2:(hx + 1) * HW2], ALU.mult),
              waits=[(sC, (e1 if hx == 0 else e2)), (qMT, mt_q)] +
                    ([(tM, st_p2["av_done_t"][par])] if hx == 0 else []))
            at_half.append(vC.n)
        st_p2["at_ms"][h] = tuple(at_half)
        st_p2["mask_done"][par] = vC.n

    def p2_av(qt, h):
        bi = qt & 1
        par = h & 1
        (at1, at2) = st_p2["at_ms"][h]
        pav = p_av[par]
        for jj in range(NJ):
            ktile = HT + qt + J[jj]
            fw = ()
            if jj == 0:
                fw = [(vC, at1), (vC, st_p2["av_free"][par])]
            elif jj == NJ // 2:
                fw = [(vC, at2)]
            T(lambda t, jj=jj, ktile=ktile, h=h, pav=pav: t.matmul(
                pav[:, 0:65],
                s_at[par][:, jj * 128:(jj + 1) * 128],
                V4[:, ktile, h, :],
                start=(jj == 0), stop=(jj == NJ - 1)),
              incs=[(tM, 1)] if jj == NJ - 1 else (),
              waits=fw)
        av_ms = tM.n
        st_p2["av_done_t"][par] = av_ms
        V(lambda v, pav=pav: v.reciprocal(s_rd[:, :], pav[:, 64:65]),
          waits=[(tM, av_ms)] +
                ([(qGT[bi], st_p2["gld"][bi]), (qGD[bi], st_p2["gd_store"][bi])] if h == 0 else []))
        V(lambda v, h=h, bi=bi, pav=pav: v.scalar_tensor_tensor(
            s_gd[bi][:, h * 64:(h + 1) * 64], pav[:, 0:64], s_rd[:, :],
            s_gtt[bi][:, h * 64:(h + 1) * 64], ALU.mult, ALU.mult))
        st_p2["av_free"][par] = vC.n

    V(lambda v: v.memset(s_nbp[:, :], 0.0))
    qG4 = DS("qG4")
    p2_tile_loads(0)
    for grp in range(4):
        p2_passA_grp(0, grp)
    p2_passA_tail(0)
    for qt in range(QT_N):
        bi = qt & 1
        for h in range(H):
            p2_passB_mm(qt, h)
            if h >= 1:
                p2_av(qt, h - 1)
            p2_mask(qt, h)
            if qt + 1 < QT_N:
                if h == 0:
                    p2_tile_loads(qt + 1)
                elif 1 <= h <= 4:
                    p2_passA_grp(qt + 1, h - 1)
                elif h == 5:
                    p2_passA_tail(qt + 1)
        p2_av(qt, H - 1)
        gd_ms = vC.n
        DA(qGD[bi], lambda s, qt=qt, bi=bi: s.dma_start(d_gated.ap()[qt], s_gd[bi][:, :]),
           waits=[(vC, gd_ms)])
        st_p2["gd_store"][bi] = qGD[bi].n

    if tap is not None and tap[0] == "gated":
        barrier()
        DA(qC, lambda s: s.dma_start(tap_t.ap(), d_gated.ap()))
    barrier()
    st2.close()
    st_kv.close()
    if upto == 2:
        _finish(nc, P, _dma_sems)
        ctx.close()
        return nc

    st3 = ExitStack()
    s_xn2 = sb("s_xn2", [128, 8 * OWN], BF16, st3)
    xn23 = s_xn2[:, :].rearrange("p (c t) -> p c t", c=8)
    st3b = ExitStack()
    s_g4 = sb("s_g4", [128, 4 * D], BF16, st3b)
    g43 = s_g4[:, :].rearrange("p (n d) -> p n d", n=4)
    s_wo = sb("s_wo", [128, 8 * 1024], BF16, st3b)
    wo3 = s_wo[:, :].rearrange("p (k o) -> p k o", k=8)
    s_wo32 = sb("s_wo32", [128, 4 * 1024], F32, st3b)
    s_gT = sb("s_gT", [128, 8 * 512], BF16, st3b)
    gT3 = s_gT[:, :].rearrange("p (c t) -> p c t", c=8)
    s_xnb3_ = sb("s_xnb3", [128, 8 * 512], BF16, st3b)
    xnb33 = s_xnb3_[:, :].rearrange("p (c t) -> p c t", c=8)
    s_r1 = sb("s_r1", [128, 8 * 512], F32, st3b)
    r13 = s_r1[:, :].rearrange("p (c t) -> p c t", c=8)
    s_r1b = sb("s_r1b", [128, 8 * 512], BF16, st3b)
    s_row3 = sb("s_row3", [1, 512], F32, st3b)
    r1b3 = s_r1b[:, :].rearrange("p (c t) -> p c t", c=8)
    p_t3 = [ps("p_t30", [128, 128], BF16, st3b), ps("p_t31", [128, 128], BF16, st3b)]
    p_o3 = [ps("p_o30", [128, 512], F32, st3b), ps("p_o31", [128, 512], F32, st3b)]
    p_sr = ps("p_sr", [1, 512], F32, st3b)
    p_bc3 = ps("p_bc3", [128, 512], F32, st3b)

    qXN3 = DS("qXN3")
    qR1S = DS("qR1S")
    qC2 = DS("qC2")
    for c in range(8):
        DW(qC2, lambda s, c=c: s.dma_start(s_g2[:, c:c + 1], g2v.ap()[c * 128:(c + 1) * 128].unsqueeze(1)))
        DW(qC2, lambda s, c=c: s.dma_start(s_s3[:, c:c + 1], s3v.ap()[c * 128:(c + 1) * 128].unsqueeze(1)))
    for c in range(32):
        DW(qC2, lambda s, c=c: s.dma_start(s_s1[:, c:c + 1], s1v.ap()[c * 128:(c + 1) * 128].unsqueeze(1)))
        DW(qC2, lambda s, c=c: s.dma_start(s_s2[:, c:c + 1], s2v.ap()[c * 128:(c + 1) * 128].unsqueeze(1)))
        DW(qC2, lambda s, c=c: s.dma_start(s_sh1[:, c:c + 1], sh1v.ap()[c * 128:(c + 1) * 128].unsqueeze(1)))
        DW(qC2, lambda s, c=c: s.dma_start(s_sh2[:, c:c + 1], sh2v.ap()[c * 128:(c + 1) * 128].unsqueeze(1)))
    qc2_q = qC2.n
    qWOB = DS("qWOB")
    p2_end_t = tM.n
    wo_half_v = []
    for wh in range(2):
        DSC(qWOB, lambda s, wh=wh: s.dma_start(
            s_wo32[:, :].rearrange("p (k o) -> p k o", k=4),
            wout.ap().rearrange("(k p) o -> p k o", p=128)[:, wh * 4:(wh + 1) * 4, :]),
            waits=[(tM, p2_end_t)] if wh == 0 else [(vC, vC.n)])
        wob_q = qWOB.n
        V(lambda v, wh=wh: v.tensor_copy(
            s_wo[:, wh * 4096:(wh + 1) * 4096], s_wo32[:, :]),
          waits=[(qWOB, wob_q)])
        wo_half_v.append(vC.n)
    pt_free = [0, 0]
    po_free = [0, 0]
    prev_r1_store = 0
    sr_evac_v = 0
    ti = 0
    p2_end_v = vC.n
    p2_end_s = sC.n
    for blk in range(4):
        DA(qG4, lambda s, blk=blk: s.dma_start(
            g43[:, :, :],
            d_gated.ap().rearrange("n p d -> p n d")[:, blk * 4:(blk + 1) * 4, :]),
           waits=[(tM, tM.n)] + ([(vC, p2_end_v)] if blk == 0 else []))
        g4q = qG4.n
        DA(qXN3, lambda s, blk=blk: s.dma_start(
            xnb33[:, :, :],
            xn_w[:, :, HALO + blk * 512: HALO + (blk + 1) * 512]),
           waits=[(vC, vC.n), (qXN[0], p0["xnst"][0]), (qXN[1], p0["xnst"][1])])
        xnq = qXN3.n
        for ttl in range(4):
            for c in range(8):
                pt = p_t3[ti]
                fw = [(vC, pt_free[ti])] + ([(qG4, g4q)] if (ttl == 0 and c == 0) else [])
                if blk == 0 and ttl == 0 and c == 0:
                    fw += [(vC, p2_end_v), (sC, p2_end_s)]
                T(lambda t, ttl=ttl, c=c, pt=pt: t.transpose(
                    pt[:, :], g43[:, ttl, c * 128:(c + 1) * 128], s_id[:, :]),
                  incs=[(tM, 1)], waits=fw)
                trm = tM.n
                V(lambda v, ttl=ttl, c=c, pt=pt: v.tensor_copy(
                    gT3[:, c, ttl * 128:(ttl + 1) * 128], pt[:, :]),
                  waits=[(tM, trm)])
                pt_free[ti] = vC.n
                ti ^= 1
        gT_ms = vC.n
        for oc in range(8):
            po = p_o3[oc & 1]
            fw = [(vC, max(po_free[oc & 1], gT_ms))] + \
                 ([(vC, wo_half_v[0])] if (blk == 0 and oc == 0) else [])
            fw4 = [(vC, wo_half_v[1])] if (blk == 0 and oc == 0) else ()
            for kc in range(8):
                T(lambda t, kc=kc, po=po, oc=oc: t.matmul(
                    po[:, :], wo3[:, kc, oc * 128:(oc + 1) * 128], gT3[:, kc, :],
                    start=(kc == 0), stop=(kc == 7)),
                  incs=[(tM, 1)] if kc == 7 else (),
                  waits=fw if kc == 0 else (fw4 if kc == 4 else ()))
            omm = tM.n
            V(lambda v, oc=oc, po=po: v.tensor_tensor(
                r13[:, oc, :], po[:, :], xnb33[:, oc, :], ALU.add),
              waits=[(tM, omm), (qXN3, xnq)] + ([(qR1S, prev_r1_store)] if oc == 0 else []))
            po_free[oc & 1] = vC.n
        r1_ms = vC.n
        DA(qR1S, lambda s, blk=blk: s.dma_start(
            d_r1.ap().rearrange("c p t -> p c t")[:, :, blk * 512:(blk + 1) * 512],
            r13[:, :, :]),
           waits=[(vC, r1_ms)])
        prev_r1_store = qR1S.n
        V(lambda v: v.tensor_copy(s_r1b[:, :], s_r1[:, :]))
        V(lambda v: v.tensor_mul(s_r1b[:, :], s_r1b[:, :], s_r1b[:, :]))
        sq_ms = vC.n
        for c in range(8):
            T(lambda t, c=c: t.matmul(p_sr[:, :], s_onescb[:, :], r1b3[:, c, :],
                                      start=(c == 0), stop=(c == 7)),
              incs=[(tM, 1)] if c == 7 else (),
              waits=[(vC, max(sq_ms, sr_evac_v))] if c == 0 else ())
        srm = tM.n
        V(lambda v: v.tensor_copy(s_row3[:, :], p_sr[:, :]), waits=[(tM, srm)])
        sr_evac_v = vC.n
        V(lambda v: v.tensor_scalar_mul(s_row3[:, :], s_row3[:, :], 1.0 / D))
        vr = vC.n
        SE(lambda s: s.activation(s_row3[:, :], s_row3[:, :], AF.Ln, bias=s_cv[0:1, 1:2]), waits=[(vC, vr)])
        SE(lambda s: s.activation(s_row3[:, :], s_row3[:, :], AF.Exp, scale=s_cv[0:1, 2:3]))
        rr_ms = sC.n
        T(lambda t: t.matmul(p_bc3[:, :], s_onesf[:, :], s_row3[:, :], start=True, stop=True),
          incs=[(tM, 1)], waits=[(sC, rr_ms), (vC, vC.n)])
        bc_ms = tM.n
        for c in range(8):
            V(lambda v, c=c, blk=blk: v.scalar_tensor_tensor(
                xn23[:, c, blk * 512:(blk + 1) * 512], r13[:, c, :], s_g2[:, c:c + 1],
                p_bc3[:, :], ALU.mult, ALU.mult),
              waits=([(tM, bc_ms)] + ([(qC2, qc2_q)] if blk == 0 else [])) if c == 0 else ())

    barrier()
    st3b.close()
    if upto == 3:
        _finish(nc, P, _dma_sems)
        st3.close()
        ctx.close()
        return nc

    st4 = ExitStack()
    s_hatt = sb("s_hatt", [128, 22 * OWN], BF16, st4)
    ha3 = s_hatt[:, :].rearrange("p (c t) -> p c t", c=22)
    st4w = ExitStack()
    s_wga = sb("s_wga", [128, 8 * FFA], BF16, st4w)
    wga3 = s_wga[:, :].rearrange("p (k o) -> p k o", k=8)
    p_f = [ps("p_f0", [128, 2048], F32, st4), ps("p_f1", [128, 2048], F32, st4)]
    pf_free_v = [0, 0]
    pf_free_s = [0, 0]

    wfg_v = wfg.ap().rearrange("(k p) o -> p k o", p=128)
    wfu_v = wfu.ap().rearrange("(k p) o -> p k o", p=128)
    qR14 = DS("qR14")
    qR2S = DS("qR2S")
    QB4 = [(0, 6), (6, 12), (12, 17), (17, 22)]
    qW4 = DS("qW4")
    wgaq_q = []
    for (a, b) in QB4:
        DW(qW4, lambda g, a=a, b=b: g.dma_start(
            wga3[:, :, a * 128:b * 128], wfg_v[:, :, a * 128:b * 128]),
           waits=([(tM, tM.n)] if a == 0 else []) + [(qW4, qW4.n)])
        wgaq_q.append(qW4.n)
    def _q4(oc):
        for qi, (a, b) in enumerate(QB4):
            if oc < b:
                return qi
    wfg_oc_t = {}
    for oc in range(22):
        pf = p_f[oc & 1]
        fw = [(sC, pf_free_s[oc & 1]), (qW4, wgaq_q[_q4(oc)])]
        for kc in range(8):
            for b in range(4):
                T(lambda t, oc=oc, kc=kc, b=b, pf=pf: t.matmul(
                    pf[:, b * 512:(b + 1) * 512],
                    wga3[:, kc, oc * 128:(oc + 1) * 128],
                    xn23[:, kc, b * 512:(b + 1) * 512],
                    start=(kc == 0), stop=(kc == 7)),
                  incs=[(tM, 1)] if (kc == 7 and b == 3) else (),
                  waits=fw if (kc == 0 and b == 0) else ())
        fm = tM.n
        wfg_oc_t[oc] = fm
        SE(lambda s, oc=oc, pf=pf: s.activation(ha3[:, oc, :], pf[:, :], AF.Silu),
           waits=[(tM, fm)])
        pf_free_s[oc & 1] = sC.n
    g_done_t = tM.n
    wuq_q = []
    for (a, b) in QB4:
        DW(qW4, lambda g, a=a, b=b: g.dma_start(
            wga3[:, :, a * 128:b * 128], wfu_v[:, :, a * 128:b * 128]),
           waits=[(tM, wfg_oc_t[b - 1]), (qW4, qW4.n)])
        wuq_q.append(qW4.n)
    for oc in range(22):
        pf = p_f[oc & 1]
        fw = [(vC, pf_free_v[oc & 1]), (sC, sC.n), (qW4, wuq_q[_q4(oc)])]
        for kc in range(8):
            for b in range(4):
                T(lambda t, oc=oc, kc=kc, b=b, pf=pf: t.matmul(
                    pf[:, b * 512:(b + 1) * 512],
                    wga3[:, kc, oc * 128:(oc + 1) * 128],
                    xn23[:, kc, b * 512:(b + 1) * 512],
                    start=(kc == 0), stop=(kc == 7)),
                  incs=[(tM, 1)] if (kc == 7 and b == 3) else (),
                  waits=fw if (kc == 0 and b == 0) else ())
        fm = tM.n
        V(lambda v, oc=oc, pf=pf: v.tensor_tensor(ha3[:, oc, :], ha3[:, oc, :], pf[:, :], ALU.mult),
          waits=[(tM, fm)])
        pf_free_v[oc & 1] = vC.n
    u_done_t = tM.n
    u_done_v = vC.n
    barrier()
    st4w.close()

    s_wd = sb("s_wd", [128, 22 * D], BF16, st4)
    wd3 = s_wd[:, :].rearrange("p (k o) -> p k o", k=22)
    s_r1s = sb("s_r1s", [128, OWN], F32, st4)
    s_r2s = sb("s_r2s", [128, OWN], F32, st4)
    wfd_v = wfd.ap().rearrange("(k p) o -> p k o", p=128)
    wdq_q = []
    for q in range(4):
        DW(qW4, lambda g, q=q: g.dma_start(
            wd3[:, :, q * 256:(q + 1) * 256], wfd_v[:, :, q * 256:(q + 1) * 256]),
           waits=([(tM, u_done_t)] if q == 0 else []) + [(qW4, qW4.n)])
        wdq_q.append(qW4.n)
    for oc in range(8):
        pf = p_f[oc & 1]
        DA(qR14, lambda s, oc=oc: s.dma_start(s_r1s[:, :], d_r1.ap()[oc]),
           waits=[(vC, vC.n)])
        r1q = qR14.n
        fw = [(vC, max(pf_free_v[oc & 1], u_done_v)), (qW4, wdq_q[oc // 2])]
        for kc in range(22):
            for b in range(4):
                T(lambda t, oc=oc, kc=kc, b=b, pf=pf: t.matmul(
                    pf[:, b * 512:(b + 1) * 512],
                    wd3[:, kc, oc * 128:(oc + 1) * 128],
                    ha3[:, kc, b * 512:(b + 1) * 512],
                    start=(kc == 0), stop=(kc == 21)),
                  incs=[(tM, 1)] if (kc == 21 and b == 3) else (),
                  waits=fw if (kc == 0 and b == 0) else ())
        fm = tM.n
        V(lambda v, pf=pf: v.tensor_tensor(s_r2s[:, :], pf[:, :], s_r1s[:, :], ALU.add),
          waits=[(tM, fm), (qR14, r1q), (qR2S, qR2S.n)])
        pf_free_v[oc & 1] = vC.n
        vv = vC.n
        DA(qR2S, lambda s, oc=oc: s.dma_start(d_r2.ap()[oc], s_r2s[:, :]), waits=[(vC, vv)])

    barrier()
    st4.close()
    st3.close()
    if upto == 4:
        _finish(nc, P, _dma_sems)
        ctx.close()
        return nc

    st5 = ExitStack()
    s_ha5 = sb("s_ha5", [128, 32 * 1024], BF16, st5)
    h53 = s_ha5[:, :].rearrange("p (c t) -> p c t", c=32)
    s_hb5 = sb("s_hb5", [128, 32 * 1024], BF16, st5)
    hb53 = s_hb5[:, :].rearrange("p (c t) -> p c t", c=32)
    xh3 = s_hb5[:, 0:8 * 1024].rearrange("p (c t) -> p c t", c=8)
    s_hs = [sb("s_hs0", [128, 32 * 256], BF16, st5), sb("s_hs1", [128, 32 * 256], BF16, st5)]
    s_rr = sb("s_rr", [128, 1024], F32, st5)
    s_xx = sb("s_xx", [128, 1024], F32, st5)
    s_oo = sb("s_oo", [128, 1024], F32, st5)
    p_5 = [ps("p_50", [128, 1024], F32, st5), ps("p_51", [128, 1024], F32, st5),
           ps("p_52", [128, 1024], F32, st5), ps("p_53", [128, 1024], F32, st5)]

    w1_v = w1T.ap().rearrange("(k p) o -> p k o", p=128)
    w2_v = w2T.ap().rearrange("(k p) o -> p k o", p=128)
    w3_v = w3T.ap().rearrange("(k p) o -> p k o", p=128)
    xTv2 = xT.ap().rearrange("(c p) t -> p c t", p=128)
    out_v = out_t.ap().rearrange("(c p) t -> p c t", p=128)

    wS5 = [DS("wS50"), DS("wS51")]
    qXH = DS("qXH")
    qRX = DS("qRX")
    qOUT = DS("qOUT")
    p5_free_v = [0] * 4
    p5_free_s = [0] * 4
    hs_free_t = [0, 0]
    state = {"fm": 0}

    def gemm_slab(wv, n_kc, src3, n_oc, evac, gate=()):
        gate = list(gate)
        for og in range(n_oc // 2):
            sl3 = s_hs[og & 1][:, 0:n_kc * 256].rearrange("p (k o) -> p k o", k=n_kc)
            DW(wS5[og & 1], lambda g, og=og, wv=wv, sl3=sl3: g.dma_start(
                sl3[:, :, :], wv[:, :, og * 256:(og + 1) * 256]),
               waits=[(tM, hs_free_t[og & 1])])
            slq = wS5[og & 1].n
            for o2 in range(2):
                oc = og * 2 + o2
                pp = p_5[oc & 3]
                fw = [(wS5[og & 1], slq), (vC, p5_free_v[oc & 3]), (sC, p5_free_s[oc & 3])]
                if og == 0 and o2 == 0:
                    fw = fw + gate
                for kc in range(n_kc):
                    for b in range(2):
                        T(lambda t, o2=o2, kc=kc, b=b, pp=pp, src3=src3, sl3=sl3, n_kc=n_kc: t.matmul(
                            pp[:, b * 512:(b + 1) * 512],
                            sl3[:, kc, o2 * 128:(o2 + 1) * 128],
                            src3[:, kc, b * 512:(b + 1) * 512],
                            start=(kc == 0), stop=(kc == n_kc - 1)),
                          incs=[(tM, 1)] if (kc == n_kc - 1 and b == 1) else (),
                          waits=fw if (kc == 0 and b == 0) else ())
                state["fm"] = tM.n
                if o2 == 1:
                    hs_free_t[og & 1] = tM.n
                evac(oc, pp, tM.n)

    for hf in range(2):
        DA(qXH, lambda s, hf=hf: s.dma_start(
            xh3[:, :, :],
            xn_w[:, :, HALO + hf * 1024: HALO + (hf + 1) * 1024]),
           waits=[(tM, tM.n)])
        xhq = qXH.n
        P.q["tensor"].append(lambda t, _v=xhq: t.wait_ge(qXH.h, _v))

        def ev_w1(oc, pp, fm):
            SE(lambda s, oc=oc, pp=pp: s.activation(
                h53[:, oc, :], pp[:, :], AF.Silu, scale=s_s1[:, oc:oc + 1]),
               waits=[(tM, fm)] + ([(qC2, qc2_q)] if oc == 0 else []))
            p5_free_s[oc & 3] = sC.n

        def ev_w2(oc, pp, fm):
            V(lambda v, oc=oc, pp=pp: v.scalar_tensor_tensor(
                h53[:, oc, :], pp[:, :], s_s2[:, oc:oc + 1], h53[:, oc, :],
                ALU.mult, ALU.mult),
              waits=[(tM, fm), (sC, sC.n)])
            p5_free_v[oc & 3] = vC.n

        gemm_slab(w1_v, 8, xh3, 32, ev_w1)
        gemm_slab(w2_v, 8, xh3, 32, ev_w2)

        def mk_ev_h(dst3, ssc):
            def ev(oc, pp, fm):
                SE(lambda s, oc=oc, pp=pp: s.activation(
                    dst3[:, oc, :], pp[:, :], AF.Gelu, scale=ssc[:, oc:oc + 1]),
                   waits=[(tM, fm)])
                p5_free_s[oc & 3] = sC.n
            return ev

        gemm_slab(h1T.ap().rearrange("(k p) o -> p k o", p=128), 32, h53, 32,
                  mk_ev_h(hb53, s_sh1), gate=[(vC, vC.n), (sC, sC.n)])
        gemm_slab(h2T.ap().rearrange("(k p) o -> p k o", p=128), 32, hb53, 32,
                  mk_ev_h(h53, s_sh2), gate=[(sC, sC.n)])

        def mk_ev_w3(hf):
            def ev(oc, pp, fm):
                DA(qRX, lambda s, oc=oc, hf=hf: s.dma_start(
                    s_rr[:, :], d_r2.ap()[oc][:, hf * 1024:(hf + 1) * 1024]),
                   waits=[(vC, vC.n)])
                DA(qRX, lambda s, oc=oc, hf=hf: s.dma_start(
                    s_xx[:, :], xTv2[:, oc, HALO + hf * 1024: HALO + (hf + 1) * 1024]),
                   waits=[(vC, vC.n)])
                xq5 = qRX.n
                V(lambda v, oc=oc, pp=pp: v.scalar_tensor_tensor(
                    s_oo[:, :], pp[:, :], s_s3[:, oc:oc + 1], s_rr[:, :], ALU.mult, ALU.add),
                  waits=[(tM, fm), (qRX, xq5), (qOUT, qOUT.n)])
                V(lambda v: v.tensor_add(s_oo[:, :], s_oo[:, :], s_xx[:, :]))
                p5_free_v[oc & 3] = vC.n
                vv = vC.n
                DA(qOUT, lambda s, oc=oc, hf=hf: s.dma_start(
                    out_v[:, oc, hf * 1024:(hf + 1) * 1024], s_oo[:, :]),
                   waits=[(vC, vv)])
            return ev

        gemm_slab(w3_v, 32, h53, 8, mk_ev_w3(hf), gate=[(sC, sC.n)])

    _finish(nc, P, _dma_sems)
    st5.close()
    ctx.close()
    return nc


_NC_CACHE = {}


def kernel(**inputs):
    x = np.asarray(inputs["L0_attn_x"], dtype=np.float32)
    offsets = np.asarray(inputs["L0_attn_offsets"]).astype(np.int64)
    offs = sorted(int(o) for o in offsets.tolist())
    OMAX = max(offs)
    HALO = ((OMAX + 127) // 128) * 128

    key = (tuple(offs),)
    if key not in _NC_CACHE:
        _NC_CACHE[key] = build_nc(offsets)
    nc = _NC_CACHE[key]

    Bsz, N, Dm = x.shape
    per_b = NCORES // Bsz
    assert N // per_b == OWN and Dm == D

    def cvt(a):
        return np.ascontiguousarray(np.asarray(a, dtype=np.float32))

    shared = {
        "wqkv": cvt(inputs["L0_attn_w_qkv"]),
        "wgate": cvt(inputs["L0_attn_w_gate"]),
        "wout": cvt(inputs["L0_attn_w_out"]),
        "wfg": cvt(inputs["L0_attn_w_ffn_gate"]),
        "wfu": cvt(inputs["L0_attn_w_ffn_up"]),
        "wfd": cvt(inputs["L0_attn_w_ffn_down"]),
        "w1T": cvt(np.asarray(inputs["L0_ffn_w1_w"], dtype=np.float32).T),
        "w2T": cvt(np.asarray(inputs["L0_ffn_w2_w"], dtype=np.float32).T),
        "h1T": cvt(np.asarray(inputs["L0_ffn_h1_w"], dtype=np.float32).T),
        "h2T": cvt(np.asarray(inputs["L0_ffn_h2_w"], dtype=np.float32).T),
        "w3T": cvt(np.asarray(inputs["L0_ffn_w3_w"], dtype=np.float32).T),
        "g1v": cvt(inputs["L0_attn_norm1_scale"]),
        "g2v": cvt(inputs["L0_attn_norm2_scale"]),
        "bgv": cvt(inputs["L0_attn_b_gate"]),
        "s1v": cvt(inputs["L0_ffn_w1_s"]),
        "s2v": cvt(inputs["L0_ffn_w2_s"]),
        "sh1v": cvt(inputs["L0_ffn_h1_s"]),
        "sh2v": cvt(inputs["L0_ffn_h2_s"]),
        "s3v": cvt(inputs["L0_ffn_w3_s"]),
        "posb": cvt(inputs["L0_attn_pos_bias"]),
        "c_iota": np.arange(128, dtype=np.float32).reshape(128, 1),
        "c_onesf": np.ones((1, 128), dtype=np.float32),
        "c_onesb": np.ones((128, 128), dtype=BF),
        "c_onescb": np.ones((128, 1), dtype=BF),
        "c_id": np.eye(128, dtype=BF),
        "c_cv": np.array([[1.0e-5, 1.0e-6, -0.5, 0.0]], dtype=np.float32),
    }

    in_maps = []
    for c in range(NCORES):
        b = c // per_b
        s0 = (c % per_b) * OWN
        lo = s0 - HALO
        if lo >= 0:
            seg = x[b, lo:s0 + OWN, :]
        else:
            seg = np.concatenate(
                [np.zeros((-lo, D), dtype=np.float32), x[b, 0:s0 + OWN, :]], axis=0)
        m = dict(shared)
        m["xT"] = np.ascontiguousarray(seg.T)
        m["tbase"] = np.full((128, 1), float(s0), dtype=np.float32)
        in_maps.append(m)

    res = run_bass_kernel_spmd(nc, in_maps, core_ids=list(range(NCORES)))

    out = np.empty((Bsz, N, D), dtype=np.float32)
    for c in range(NCORES):
        b = c // per_b
        s0 = (c % per_b) * OWN
        out[b, s0:s0 + OWN, :] = np.asarray(res.results[c]["out"]).T
    return out


# version 14
# speedup vs baseline: 1.0877x; 1.0877x over previous
from contextlib import ExitStack

import numpy as np
import ml_dtypes
import concourse.bass as bass
import concourse.mybir as mybir
from concourse.bass_utils import run_bass_kernel_spmd

F32 = mybir.dt.float32
BF16 = mybir.dt.bfloat16
AF = mybir.ActivationFunctionType
ALU = mybir.AluOpType
AX = mybir.AxisListType
BF = ml_dtypes.bfloat16

D = 1024
H = 16
NCORES = 8
OWN = 2048
FFA = 2816
FFH = 4096
QT_N = OWN // 128


def _offset_runs(positions):
    runs = []
    i = 0
    pos = list(positions)
    n = len(pos)
    while i < n:
        if i + 1 < n:
            step = pos[i + 1] - pos[i]
            j = i + 1
            while j + 1 < n and pos[j + 1] - pos[j] == step:
                j += 1
            runs.append((i, step, j - i + 1))
            i = j + 1
        else:
            runs.append((i, 1, 1))
            i += 1
    return runs


class Ctr:
    def __init__(self, h):
        self.h = h
        self.n = 0


class Prog:
    def __init__(self):
        self.q = {"tensor": [], "vector": [], "scalar": [], "gpsimd": [], "sync": []}

    def emit(self, eng, fn, incs=(), waits=()):
        lst = self.q[eng]
        for (c, v) in waits:
            if v > 0:
                lst.append(lambda e, _h=c.h, _v=v: e.wait_ge(_h, _v))
        ivals = []
        for (c, k) in incs:
            c.n += k
            ivals.append((c.h, k))

        def run(e, _fn=fn, _iv=tuple(ivals)):
            inst = _fn(e)
            for (_h, _k) in _iv:
                inst = inst.then_inc(_h, _k)

        lst.append(run)


def _finish(nc, P, dma_sems):
    for c_ in dma_sems.values():
        if c_.n > 0:
            P.emit("sync", lambda s, _h=c_.h, _v=c_.n: s.wait_ge(_h, _v))
    with nc.Block() as blk:
        blk.tensor(lambda t: [f(t) for f in P.q["tensor"]])
        blk.vector(lambda v: [f(v) for f in P.q["vector"]])
        blk.scalar(lambda s: [f(s) for f in P.q["scalar"]])
        blk.gpsimd(lambda g: [f(g) for f in P.q["gpsimd"]])
        blk.sync(lambda s: [f(s) for f in P.q["sync"]])


def build_nc(offsets, tap=None, upto=99, p2sub=99):
    offs = sorted(int(o) for o in np.asarray(offsets).tolist())
    OMAX = max(offs)
    HALO = ((OMAX + 127) // 128) * 128
    TOT = OWN + HALO
    NB = TOT // 512
    NT = TOT // 128
    HT = HALO // 128

    Jset = set()
    for o in offs:
        Jset.add((-o) // 128)
        Jset.add((127 - o) // 128)
    J = sorted(Jset, reverse=True)
    NJ = len(J)
    assert min(J) >= -HT and NJ % 4 == 0
    GL = ((OMAX + 256 + 511) // 512) * 512

    nc = bass.Bass("TRN2", target_bir_lowering=False)

    def din(name, shape, dt=F32):
        return nc.dram_tensor(name, shape, dt, kind="ExternalInput")

    xT = din("xT", [D, TOT])
    wqkv = din("wqkv", [D, 3 * D])
    wgate = din("wgate", [D, D])
    wout = din("wout", [D, D])
    wfg = din("wfg", [D, FFA])
    wfu = din("wfu", [D, FFA])
    wfd = din("wfd", [FFA, D])
    w1T = din("w1T", [D, FFH])
    w2T = din("w2T", [D, FFH])
    h1T = din("h1T", [FFH, FFH])
    h2T = din("h2T", [FFH, FFH])
    w3T = din("w3T", [FFH, D])
    g1v = din("g1v", [D])
    g2v = din("g2v", [D])
    bgv = din("bgv", [D])
    s1v = din("s1v", [FFH])
    s2v = din("s2v", [FFH])
    sh1v = din("sh1v", [FFH])
    sh2v = din("sh2v", [FFH])
    s3v = din("s3v", [D])
    posb = din("posb", [len(offs), H])
    tbase = din("tbase", [128, 1])
    c_iota = din("c_iota", [128, 1])
    c_onesf = din("c_onesf", [1, 128])
    c_onesb = din("c_onesb", [128, 128], BF16)
    c_onescb = din("c_onescb", [128, 1], BF16)
    c_id = din("c_id", [128, 128], BF16)
    c_cv = din("c_cv", [1, 4])

    out_t = nc.dram_tensor("out", [D, OWN], F32, kind="ExternalOutput")
    tap_t = None
    if tap is not None:
        tap_t = nc.dram_tensor("tapout", list(tap[1]), tap[2], kind="ExternalOutput")

    d_xn = nc.dram_tensor("d_xn", [8, 128, TOT], BF16, kind="Internal")
    d_q = nc.dram_tensor("d_q", [8, 128, OWN], BF16, kind="Internal")
    d_gate = nc.dram_tensor("d_gate", [QT_N, 128, D], BF16, kind="Internal")
    d_gated = nc.dram_tensor("d_gated", [QT_N, 128, D], BF16, kind="Internal")
    d_r1 = nc.dram_tensor("d_r1", [8, 128, OWN], F32, kind="Internal")
    d_r2 = nc.dram_tensor("d_r2", [8, 128, OWN], F32, kind="Internal")
    d_gtab = nc.dram_tensor("d_gtab", [H, GL], BF16, kind="Internal")
    d_grep = nc.dram_tensor("d_grep", [H, 128, GL], BF16, kind="Internal")
    d_nb = nc.dram_tensor("d_nb", [2, H, 128], BF16, kind="Internal")
    d_negb = nc.dram_tensor("d_negb", [2], F32, kind="Internal")

    P = Prog()
    ctx = ExitStack()

    def sem(name):
        return Ctr(ctx.enter_context(nc.semaphore(name)))

    tM = sem("tM")
    vC = sem("vC")
    sC = sem("sC")
    _dma_sems = {}

    def DS(name):
        if name not in _dma_sems:
            _dma_sems[name] = sem(name)
        return _dma_sems[name]

    def sb(name, shape, dt, stack=None, side=None):
        if side is None:
            return (stack or ctx).enter_context(nc.sbuf_tensor(name, shape, dt))
        return (stack or ctx).enter_context(nc.sbuf_tensor(name, shape, dt, side=side))

    def ps(name, shape, dt=F32, stack=None):
        return (stack or ctx).enter_context(nc.psum_tensor(name, shape, dt))

    def T(fn, incs=(), waits=()):
        P.emit("tensor", fn, incs, waits)

    def V(fn, waits=()):
        P.emit("vector", fn, incs=[(vC, 1)], waits=[(vC, vC.n)] + list(waits))

    def SE(fn, waits=()):
        P.emit("scalar", fn, incs=[(sC, 1)], waits=[(sC, sC.n)] + list(waits))

    def DW(sm, fn, waits=()):
        P.emit("gpsimd", fn, incs=[(sm, 16)], waits=waits)

    def DA(sm, fn, waits=()):
        P.emit("sync", fn, incs=[(sm, 16)], waits=waits)

    def DSC(sm, fn, waits=()):
        P.emit("scalar", fn, incs=[(sm, 16)], waits=waits)

    def barrier():
        vals = [(tM, tM.n), (vC, vC.n), (sC, sC.n)]
        vals += [(c, c.n) for c in _dma_sems.values()]
        for eng in ("tensor", "vector", "scalar", "gpsimd", "sync"):
            for (c, v) in vals:
                if v > 0:
                    P.emit(eng, lambda e, _h=c.h, _v=v: e.wait_ge(_h, _v))

    s_onesf = sb("s_onesf", [1, 128], F32)
    s_onesb = sb("s_onesb", [128, 128], BF16)
    s_onescb = sb("s_onescb", [128, 1], BF16)
    s_id = sb("s_id", [128, 128], BF16)
    s_iota = sb("s_iota", [128, 1], F32)
    s_tb = sb("s_tb", [128, 1], F32)
    s_tvec = sb("s_tvec", [128, 1], F32)
    s_g1 = sb("s_g1", [128, 8], F32)
    s_g2 = sb("s_g2", [128, 8], F32)
    s_s1 = sb("s_s1", [128, 32], F32)
    s_s2 = sb("s_s2", [128, 32], F32)
    s_sh1 = sb("s_sh1", [128, 32], F32)
    s_sh2 = sb("s_sh2", [128, 32], F32)
    s_s3 = sb("s_s3", [128, 8], F32)
    s_bgbc = sb("s_bgbc", [128, D], F32)
    s_cv = sb("s_cv", [1, 4], F32)

    qC = DS("qC")
    DA(qC, lambda s: s.dma_start(s_onesf[:, :], c_onesf.ap()))
    DA(qC, lambda s: s.dma_start(s_onesb[:, :], c_onesb.ap()))
    DA(qC, lambda s: s.dma_start(s_onescb[:, :], c_onescb.ap()))
    DA(qC, lambda s: s.dma_start(s_id[:, :], c_id.ap()))
    DA(qC, lambda s: s.dma_start(s_iota[:, :], c_iota.ap()))
    DA(qC, lambda s: s.dma_start(s_tb[:, :], tbase.ap()))
    for c in range(8):
        DA(qC, lambda s, c=c: s.dma_start(s_g1[:, c:c + 1], g1v.ap()[c * 128:(c + 1) * 128].unsqueeze(1)))
    DA(qC, lambda s: s.dma_start(s_cv[:, :], c_cv.ap()))
    DA(qC, lambda s: s.dma_start(s_bgbc[:, :], bass.AP(bgv, 0, [[0, 128], [1, D]])))
    consts_q = qC.n

    V(lambda v: v.tensor_add(s_tvec[:, :], s_iota[:, :], s_tb[:, :]),
      waits=[(qC, consts_q)])

    st0 = ExitStack()
    s_xa = sb("s_xa", [128, 8 * TOT], BF16, st0, side="right")
    xa3 = s_xa[:, :].rearrange("p (c t) -> p c t", c=8)
    qG = DS("qG")
    stg = ExitStack()
    s_gpb = sb("s_gpb", [H, GL], F32, stg, side="right")
    s_gme = sb("s_gme", [H, GL], BF16, stg, side="right")
    V(lambda v: v.memset(s_gpb[:, :], -1.0e30))
    gpb_ms = vC.n
    for (i0, step, cnt) in _offset_runs([o + 127 for o in offs]):
        p0 = offs[i0] + 127

        def scat(s, i0=i0, step=step, cnt=cnt, p0=p0):
            src = posb.ap()[i0:i0 + cnt, :].transpose([1, 0])
            dst = s_gpb[:, p0:p0 + step * cnt:step] if cnt > 1 else s_gpb[:, p0:p0 + 1]
            return s.dma_start(dst, src, allow_slow_non_contiguous=True)
        DA(qG, scat, waits=[(vC, gpb_ms), (qC, consts_q)])
    scat_q = qG.n
    SE(lambda s: s.activation(s_gme[:, :], s_gpb[:, :], AF.Exp),
       waits=[(qG, scat_q)])
    gme_ms = sC.n
    DA(qG, lambda s: s.dma_start(d_gtab.ap(), s_gme[:, :]), waits=[(sC, gme_ms)])
    gtab_q = qG.n
    for h in range(H):
        DA(qG, lambda s, h=h: s.dma_start(
            d_grep.ap()[h], bass.AP(d_gtab, h * GL, [[0, 128], [1, GL]])),
           waits=[(qG, gtab_q)])
    grep_q = qG.n
    g_read_done = (gme_ms, gtab_q)
    stg.close()

    st1b = ExitStack()
    s_w8 = [sb("s_w80", [128, 8 * 512], BF16, st1b, side="right"),
            sb("s_w81", [128, 8 * 512], BF16, st1b, side="right")]
    w83v = [t[:, :].rearrange("p (k o) -> p k o", k=8) for t in s_w8]
    s_w832 = sb("s_w832", [128, 4 * 512], F32, st1b, side="right")
    w8323 = s_w832[:, :].rearrange("p (k o) -> p k o", k=4)
    s_evq = sb("s_evq", [128, 1024], BF16, st1b, side="right")
    s_gt32 = sb("s_gt32", [128, 512], F32, st1b, side="right")
    s_vc = sb("s_vc", [128, 1], F32, st1b, side="right")
    wSL = DS("wSL")
    qWS = DS("qWS")
    qES = DS("qES")
    p1w = {"cast_prev": 0, "mms_done": [0, 0], "idx": 0}

    def p1_slab(src_slab):
        si = p1w["idx"] & 1
        p1w["idx"] += 1
        for hf in range(2):
            DA(qWS, lambda s, src_slab=src_slab, hf=hf: s.dma_start(
                w8323[:, :, :], src_slab[:, hf * 4:(hf + 1) * 4, :]),
               waits=[(vC, p1w["cast_prev"])])
            wq = qWS.n
            V(lambda v, si=si, hf=hf: v.tensor_copy(
                s_w8[si][:, hf * 2048:(hf + 1) * 2048], s_w832[:, :]),
              waits=[(qWS, wq)] + ([(tM, p1w["mms_done"][si])] if hf == 0 else []))
            p1w["cast_prev"] = vC.n
        return si, vC.n

    wqkv_v = wqkv.ap().rearrange("(k p) o -> p k o", p=128)
    wgate_v = wgate.ap().rearrange("(k p) o -> p k o", p=128)
    k_pre = []
    st0b = ExitStack()
    s_row = sb("s_row", [1, 512 * 4], F32, st0b, side="right")
    s_x32 = [sb("s_x320", [128, 8 * 512], F32, st0b, side="right"),
             sb("s_x321", [128, 8 * 512], F32, st0b, side="right")]
    s_xb = [sb("s_xb0", [128, 8 * 512], BF16, st0b, side="right"),
            sb("s_xb1", [128, 8 * 512], BF16, st0b, side="right")]
    s_xnb = [sb("s_xnb0", [128, 8 * 512], BF16, st0b, side="right"),
             sb("s_xnb1", [128, 8 * 512], BF16, st0b, side="right")]
    p_s1 = [ps("p_s1a", [1, 512], F32, st0b), ps("p_s1b", [1, 512], F32, st0b)]
    p_s2 = [ps("p_s2a", [1, 512], F32, st0b), ps("p_s2b", [1, 512], F32, st0b)]
    p_b1 = ps("p_b1", [128, 512], F32, st0b)
    p_b2 = ps("p_b2", [128, 512], F32, st0b)
    p_b3 = ps("p_b3", [128, 512], F32, st0b)

    xTv = xT.ap().rearrange("(c p) t -> p c t", p=128)
    xn_w = d_xn.ap().rearrange("c p t -> p c t")

    def row(i):
        return s_row[:, i * 512:(i + 1) * 512]

    qX0 = [DS("qX00"), DS("qX01")]
    wXB = [DS("wXB0"), DS("wXB1")]
    qXN = [DS("qXN0"), DS("qXN1")]
    p0 = {
        "apl_v": [0, 0], "sq_v": [0, 0], "stat_tm": [0, 0], "statevac_v": [0, 0],
        "xq": [0, 0], "wq": [0, 0], "xnst": [0, 0], "bc_ms": 0, "stat_ms": [0, 0],
    }

    def p0_load(blk):
        p = blk & 1
        cslice = slice(blk * 512, (blk + 1) * 512)
        gw = [(sC, g_read_done[0]), (qG, g_read_done[1])] if blk == 0 else []
        DSC(qX0[p], lambda s, cslice=cslice, p=p: s.dma_start(
            s_x32[p][:, :].rearrange("p (c t) -> p c t", c=8), xTv[:, :, cslice]),
            waits=[(vC, p0["apl_v"][p])] + gw)
        p0["xq"][p] = qX0[p].n
        DW(wXB[p], lambda g, cslice=cslice, p=p: g.dma_start(
            s_xb[p][:, :].rearrange("p (c t) -> p c t", c=8), xTv[:, :, cslice]),
           waits=[(vC, p0["sq_v"][p]), (tM, p0["stat_tm"][p])] + gw)
        p0["wq"][p] = wXB[p].n

    def p0_stats(blk):
        p = blk & 1
        xb3 = s_xb[p][:, :].rearrange("p (c t) -> p c t", c=8)
        sq3 = s_xnb[p][:, :].rearrange("p (c t) -> p c t", c=8)
        V(lambda v, p=p: v.tensor_mul(s_xnb[p][:, :], s_xb[p][:, :], s_xb[p][:, :]),
          waits=[(wXB[p], p0["wq"][p]), (qXN[p], p0["xnst"][p])])
        sq_ms = vC.n
        p0["sq_v"][p] = sq_ms
        for c in range(8):
            T(lambda t, c=c, p=p, xb3=xb3: t.matmul(p_s1[p][:, :], s_onescb[:, :], xb3[:, c, :],
                                                    start=(c == 0), stop=(c == 7)),
              waits=[(wXB[p], p0["wq"][p]), (vC, p0["statevac_v"][p])] if c == 0 else ())
        for c in range(8):
            T(lambda t, c=c, p=p, sq3=sq3: t.matmul(p_s2[p][:, :], s_onescb[:, :], sq3[:, c, :],
                                                    start=(c == 0), stop=(c == 7)),
              incs=[(tM, 1)] if c == 7 else (),
              waits=[(vC, sq_ms)] if c == 0 else ())
        p0["stat_ms"][p] = tM.n
        p0["stat_tm"][p] = tM.n

    def p0_rows(blk):
        p = blk & 1
        V(lambda v, p=p: v.tensor_copy(row(0), p_s1[p][:, :]),
          waits=[(tM, p0["stat_ms"][p]), (tM, p0["bc_ms"])])
        V(lambda v, p=p: v.tensor_copy(row(1), p_s2[p][:, :]))
        p0["statevac_v"][p] = vC.n
        V(lambda v: v.tensor_scalar_mul(row(0), row(0), 1.0 / D))
        V(lambda v: v.tensor_mul(row(2), row(0), row(0)))
        V(lambda v: v.tensor_scalar_mul(row(1), row(1), 1.0 / D))
        V(lambda v: v.tensor_sub(row(1), row(1), row(2)))
        vvar = vC.n
        SE(lambda s: s.activation(row(2), row(1), AF.Ln, bias=s_cv[0:1, 0:1]), waits=[(vC, vvar)])
        SE(lambda s: s.activation(row(2), row(2), AF.Exp, scale=s_cv[0:1, 2:3]))
        se_rstd = sC.n
        V(lambda v: v.tensor_scalar_add(row(3), row(1), 1.0e-5), waits=[(sC, se_rstd)])
        V(lambda v: v.reciprocal(row(3), row(3)))
        V(lambda v: v.tensor_scalar(row(3), row(3), -1.0e-5, 1.0 + 1.0e-6, ALU.mult, ALU.add))
        vq_ = vC.n
        SE(lambda s: s.activation(row(3), row(3), AF.Ln), waits=[(vC, vq_)])
        SE(lambda s: s.activation(row(3), row(3), AF.Exp, scale=s_cv[0:1, 2:3]))
        se_rr = sC.n
        V(lambda v: v.tensor_scalar_mul(row(0), row(0), -1.0))
        V(lambda v: v.tensor_mul(row(3), row(3), row(2)), waits=[(sC, se_rr)])
        vsm = vC.n
        T(lambda t: t.matmul(p_b1[:, :], s_onesf[:, :], row(0), start=True, stop=True),
          waits=[(vC, vsm), (sC, se_rr)])
        T(lambda t: t.matmul(p_b2[:, :], s_onesf[:, :], row(2), start=True, stop=True))
        T(lambda t: t.matmul(p_b3[:, :], s_onesf[:, :], row(3), start=True, stop=True),
          incs=[(tM, 1)])
        p0["bc_ms"] = tM.n

    def p0_apply(blk):
        p = blk & 1
        cslice = slice(blk * 512, (blk + 1) * 512)
        x323 = s_x32[p][:, :].rearrange("p (c t) -> p c t", c=8)
        xnb3 = s_xnb[p][:, :].rearrange("p (c t) -> p c t", c=8)
        bc = p0["bc_ms"]
        for c in range(8):
            V(lambda v, c=c, x323=x323: v.tensor_tensor(
                x323[:, c, :], x323[:, c, :], p_b1[:, :], ALU.add),
              waits=[(tM, bc), (qX0[p], p0["xq"][p])] if c == 0 else ())
            V(lambda v, c=c, x323=x323, xnb3=xnb3: v.tensor_tensor(
                xnb3[:, c, :], x323[:, c, :], p_b2[:, :], ALU.mult))
            V(lambda v, c=c, cslice=cslice, x323=x323: v.scalar_tensor_tensor(
                xa3[:, c, cslice], x323[:, c, :], s_g1[:, c:c + 1], p_b3[:, :],
                ALU.mult, ALU.mult))
        apl_ms = vC.n
        p0["apl_v"][p] = apl_ms
        DW(qXN[p], lambda s, cslice=cslice, p=p: s.dma_start(
            xn_w[:, :, cslice], s_xnb[p][:, :].rearrange("p (c t) -> p c t", c=8)),
           waits=[(vC, apl_ms)])
        p0["xnst"][p] = qXN[p].n

    p0_load(0)
    p0_stats(0)
    for blk in range(NB):
        if blk + 1 < NB:
            p0_load(blk + 1)
        p0_rows(blk)
        if blk + 1 < NB:
            p0_stats(blk + 1)
        p0_apply(blk)
        if blk <= 1:
            k_pre.append(p1_slab(wqkv_v[:, :, D + blk * 512: D + (blk + 1) * 512]))

    barrier()
    st0b.close()
    if upto == 0:
        _finish(nc, P, _dma_sems)
        st0.close()
        ctx.close()
        return nc

    st_kv = ExitStack()
    s_K = sb("s_K", [128, 8 * TOT], BF16, st_kv)
    K3 = s_K[:, :].rearrange("p (c t) -> p c t", c=8)
    s_V = sb("s_V", [128, NT * H * 65], BF16, st_kv)
    V4 = s_V[:, :].rearrange("p (n h e) -> p n h e", n=NT, h=H)
    pg = [ps("p_gm0", [128, 1024], F32, st1b), ps("p_gm1", [128, 1024], F32, st1b)]
    pg_free = [0, 0]
    pi = 0

    for (base_col, is_q) in ((D, False), (0, True)):
        for wh in range(2):
            if not is_q:
                si, cast_v = k_pre[wh]
            else:
                si, cast_v = p1_slab(wqkv_v[:, :, base_col + wh * 512: base_col + (wh + 1) * 512])
            w83 = w83v[si]
            grps = [(3, 2), (5, 2)] if is_q else [(0, 2), (2, 2), (4, 2), (6, 1)]
            for oc4 in range(4):
                oc = wh * 4 + oc4
                for (b0, gn) in grps:
                    p = pg[pi]
                    fw = [(vC, pg_free[pi])] + ([(vC, cast_v)] if (oc4 == 0 and b0 == grps[0][0]) else [])
                    for kc in range(8):
                        for b in range(gn):
                            T(lambda t, oc4=oc4, kc=kc, b0=b0, b=b, p=p, w83=w83: t.matmul(
                                p[:, b * 512:(b + 1) * 512],
                                w83[:, kc, oc4 * 128:(oc4 + 1) * 128],
                                xa3[:, kc, (b0 + b) * 512:(b0 + b + 1) * 512],
                                start=(kc == 0), stop=(kc == 7)),
                              incs=[(tM, 1)] if (kc == 7 and b == gn - 1) else (),
                              waits=fw if (kc == 0 and b == 0) else ())
                    mm_ms = tM.n
                    if is_q:
                        V(lambda v, p=p: v.tensor_scalar_mul(s_evq[:, :], p[:, :], 0.125),
                          waits=[(tM, mm_ms), (qES, qES.n)])
                        ev_ms = vC.n
                        DA(qES, lambda s, oc=oc, b0=b0: s.dma_start(
                            d_q.ap()[oc][:, (b0 - 3) * 512:(b0 - 1) * 512], s_evq[:, :]),
                           waits=[(vC, ev_ms)])
                    else:
                        V(lambda v, oc=oc, b0=b0, gn=gn, p=p: v.tensor_copy(
                            K3[:, oc, b0 * 512:(b0 + gn) * 512], p[:, 0:gn * 512]),
                          waits=[(tM, mm_ms)])
                        ev_ms = vC.n
                    pg_free[pi] = ev_ms
                    pi ^= 1
            p1w["mms_done"][si] = tM.n

    for wh in range(2):
        si, cast_v = p1_slab(wgate_v[:, :, wh * 512:(wh + 1) * 512])
        w83 = w83v[si]
        for tt in range(QT_N):
            p = pg[pi]
            fw = [(vC, pg_free[pi])] + ([(vC, cast_v)] if tt == 0 else [])
            for kc in range(8):
                T(lambda t, tt=tt, kc=kc, p=p, w83=w83: t.matmul(
                    p[:, 0:512],
                    xa3[:, kc, HALO + tt * 128: HALO + (tt + 1) * 128],
                    w83[:, kc, :],
                    start=(kc == 0), stop=(kc == 7)),
                  incs=[(tM, 1)] if kc == 7 else (),
                  waits=fw if kc == 0 else ())
            mm_ms = tM.n
            V(lambda v, p=p, wh=wh: v.tensor_tensor(
                s_gt32[:, :], p[:, 0:512], s_bgbc[:, wh * 512:(wh + 1) * 512], ALU.add),
              waits=[(tM, mm_ms), (sC, sC.n)])
            pg_free[pi] = vC.n
            pi ^= 1
            vv = vC.n
            SE(lambda s: s.activation(s_evq[:, 0:512], s_gt32[:, :], AF.Sigmoid),
               waits=[(vC, vv), (qES, qES.n)])
            sv = sC.n
            DA(qES, lambda s, tt=tt, wh=wh: s.dma_start(
                d_gate.ap()[tt][:, wh * 512:(wh + 1) * 512], s_evq[:, 0:512]),
               waits=[(sC, sv)])
        p1w["mms_done"][si] = tM.n

    for wh in range(2):
        si, cast_v = p1_slab(wqkv_v[:, :, 2 * D + wh * 512: 2 * D + (wh + 1) * 512])
        w83 = w83v[si]
        for tt in range(NT):
            p = pg[pi]
            fw = [(vC, pg_free[pi])] + ([(vC, cast_v)] if tt == 0 else [])
            for kc in range(8):
                T(lambda t, tt=tt, kc=kc, p=p, w83=w83: t.matmul(
                    p[:, 0:512],
                    xa3[:, kc, tt * 128:(tt + 1) * 128],
                    w83[:, kc, :],
                    start=(kc == 0), stop=(kc == 7)),
                  incs=[(tM, 1)] if kc == 7 else (),
                  waits=fw if kc == 0 else ())
            mm_ms = tM.n
            V(lambda v, tt=tt, p=p, wh=wh: v.tensor_copy(
                V4[:, tt, wh * 8:(wh + 1) * 8, 0:64],
                p[:, 0:512].rearrange("p (h d) -> p h d", h=8)),
              waits=[(tM, mm_ms)])
            V(lambda v, tt=tt: v.tensor_single_scalar(
                s_vc[:, :], s_tvec[:, :], float(HALO - tt * 128), ALU.is_ge))
            V(lambda v, tt=tt, wh=wh: v.tensor_copy(
                V4[:, tt, wh * 8:(wh + 1) * 8, 64:65],
                s_vc[:, :].unsqueeze(1).broadcast_to((128, 8, 1))))
            pg_free[pi] = vC.n
            pi ^= 1

    barrier()
    st1b.close()
    st0.close()
    if upto == 1:
        _finish(nc, P, _dma_sems)
        st_kv.close()
        ctx.close()
        return nc

    qMT = DS("qMT")
    qQT = [DS("qQT0"), DS("qQT1")]
    qGT = [DS("qGT0"), DS("qGT1")]
    qGD = [DS("qGD0"), DS("qGD1")]
    qNB = [DS("qNB0"), DS("qNB1")]
    st2 = ExitStack()
    s_MT = sb("s_MT", [128, H * NJ * 128], BF16, st2)
    MT3 = s_MT[:, :].rearrange("p (h x) -> p h x", h=H)
    MT4 = s_MT[:, :].rearrange("p (h j q) -> p h j q", h=H, j=NJ)
    _offs_mt = [127 - 128 * J[jj] for jj in range(NJ)]
    for h in range(H):
        for (i0, stp, cnt_) in _offset_runs(_offs_mt):
            def mdma(s, h=h, i0=i0, stp=stp, cnt_=cnt_):
                src = bass.AP(d_grep, h * 128 * GL + _offs_mt[i0],
                              [[GL - 1, 128], [stp, cnt_], [1, 128]])
                return s.dma_start(MT4[:, h, i0:i0 + cnt_, :], src)
            DA(qMT, mdma, waits=[(qG, grep_q)] if h == 0 else ())
    mt_q = qMT.n

    s_qt = [sb("s_qt0", [128, 8 * 128], BF16, st2), sb("s_qt1", [128, 8 * 128], BF16, st2)]
    s_gtt = [sb("s_gtt0", [128, D], BF16, st2), sb("s_gtt1", [128, D], BF16, st2)]
    s_gd = [sb("s_gd0", [128, D], BF16, st2), sb("s_gd1", [128, D], BF16, st2)]
    s_b = [sb("s_b0", [128, 16], F32, st2), sb("s_b1", [128, 16], F32, st2)]
    s_bq = sb("s_bq", [128, 1], F32, st2)
    s_nbm = sb("s_nbm", [1, 1], F32, st2)
    s_negB = [sb("s_negB0", [128, 1], F32, st2), sb("s_negB1", [128, 1], F32, st2)]
    s_E = [sb("s_E0", [128, NJ * 128], BF16, st2), sb("s_E1", [128, NJ * 128], BF16, st2)]
    s_at = [sb("s_at0", [128, NJ * 128], BF16, st2), sb("s_at1", [128, NJ * 128], BF16, st2)]
    s_rd = sb("s_rd", [128, 1], F32, st2)
    p_A = [ps("p_A0", [128, 256], F32, st2), ps("p_A1", [128, 256], F32, st2)]
    p_B = [ps("p_B0", [128, NJ * 128], F32, st2), ps("p_B1", [128, NJ * 128], F32, st2)]
    p_av = [ps("p_av0", [128, 65], F32, st2), ps("p_av1", [128, 65], F32, st2)]
    s_nbp = sb("s_nbp", [128, 128], BF16, st2)
    s_nbTp = sb("s_nbTp", [128, 128], BF16, st2)

    bstride = 1 - min(J)
    st_p2 = {
        "pB_free": [0, 0],
        "mask_done": [0, 0],
        "av_done_t": [0, 0],
        "at_ms": {},
        "pb_ms": {},
        "exp_ms": {},
        "qld": [0, 0],
        "gld": [0, 0],
        "negb_q": [0, 0],
        "gd_store": [0, 0],
        "pA_guard": 0,
        "av_free": [0, 0],
        "exp_done": [0, 0],
    }

    def p2_tile_loads(qt):
        bi = qt & 1
        DA(qQT[bi], lambda s, qt=qt, bi=bi: s.dma_start(
            s_qt[bi][:, :].rearrange("p (c t) -> p c t", c=8),
            d_q.ap().rearrange("c p t -> p c t")[:, :, qt * 128:(qt + 1) * 128]),
           waits=[(tM, tM.n)])
        st_p2["qld"][bi] = qQT[bi].n
        DA(qGT[bi], lambda s, qt=qt, bi=bi: s.dma_start(s_gtt[bi][:, :], d_gate.ap()[qt]),
           waits=[(vC, vC.n)])
        st_p2["gld"][bi] = qGT[bi].n

    def p2_passA_grp(qt, grp):
        bi = qt & 1
        q3 = s_qt[bi][:, :].rearrange("p (c t) -> p c t", c=8)
        t0col = HALO + qt * 128
        start_col = t0col + 128 - bstride * 128
        for hh in range(4):
            h = grp * 4 + hh
            g, r0 = h // 2, (h % 2) * 64
            par2 = h % 2
            fw = []
            if hh == 0:
                fw = [(vC, st_p2["pA_guard"]), (qQT[bi], st_p2["qld"][bi])]
            T(lambda t, hh=hh, g=g, r0=r0, par2=par2, start_col=start_col, q3=q3: t.matmul(
                p_A[par2][:, (hh // 2) * 128:(hh // 2 + 1) * 128],
                q3[r0:r0 + 64, g, :],
                K3[r0:r0 + 64, g, start_col:start_col + bstride * 128:bstride],
                start=(hh < 2), stop=(hh >= 2),
                tile_position=(r0, 0)),
              incs=[(tM, 1)] if hh == 3 else (),
              waits=fw if hh == 0 else ())
        amss = tM.n
        for par2 in range(2):
            V(lambda v, qt=qt, bi=bi, grp=grp, par2=par2: v.tensor_reduce(
                s_b[bi][:, grp * 4 + par2: (grp + 1) * 4: 2],
                p_A[par2][:, :].rearrange("p (h k) -> p h k", h=2),
                axis=AX.X, op=ALU.max),
              waits=[(tM, amss)] if par2 == 0 else ())
        st_p2["pA_guard"] = vC.n

    def p2_passA_tail(qt):
        bi = qt & 1
        V(lambda v, bi=bi: v.tensor_reduce(s_bq[:, :], s_b[bi][:, :], axis=AX.X, op=ALU.max),
          waits=[(qNB[bi ^ 1], qNB[bi ^ 1].n), (qNB[bi], qNB[bi].n)])
        V(lambda v: v.tensor_scalar_mul(s_nbp[:, 0:1], s_bq[:, :], -1.0))
        nb_ms = vC.n
        DA(qNB[bi], lambda s: s.dma_start(s_nbTp[:, :], s_nbp[:, :], transpose=True),
           waits=[(vC, nb_ms)])
        nbq = qNB[bi].n
        V(lambda v, nbq=nbq: v.tensor_reduce(
            s_nbm[:, :], s_nbTp[0:1, :], axis=AX.X, op=ALU.min),
          waits=[(qNB[bi], nbq)])
        nbm_v = vC.n
        DA(qNB[bi], lambda s, bi=bi: s.dma_start(d_negb.ap()[bi:bi + 1].unsqueeze(0), s_nbm[:, :]),
           waits=[(vC, nbm_v)])
        DA(qNB[bi], lambda s, bi=bi: s.dma_start(
            s_negB[bi][:, :], bass.AP(d_negb, bi, [[0, 128], [1, 1]])),
           waits=[(qNB[bi], qNB[bi].n), (sC, sC.n)])
        st_p2["negb_q"][bi] = qNB[bi].n

    HW2 = (NJ * 128) // 2

    def p2_passB_mm(qt, h):
        bi = qt & 1
        par = h & 1
        g, r0 = h // 2, (h % 2) * 64
        pB = p_B[par]
        q3 = s_qt[bi][:, :].rearrange("p (c t) -> p c t", c=8)
        fw = [(sC, st_p2["exp_done"][par])] + \
             ([(qQT[bi], st_p2["qld"][bi])] if h == 0 else [])
        for jj in range(NJ):
            ktile = HT + qt + J[jj]
            T(lambda t, jj=jj, ktile=ktile, g=g, r0=r0, pB=pB, q3=q3: t.matmul(
                pB[:, jj * 128:(jj + 1) * 128],
                K3[r0:r0 + 64, g, ktile * 128:(ktile + 1) * 128],
                q3[r0:r0 + 64, g, :],
                start=(jj % (NJ // 2) == 0), stop=(jj % 4 == 3),
                tile_position=(r0, 0)),
              incs=[(tM, 1)] if jj == NJ - 1 else (),
              waits=fw if jj == 0 else ())
        st_p2["pb_ms"][h] = tM.n
        eh = []
        for hx in range(2):
            SE(lambda s, pB=pB, par=par, hx=hx, bi=bi: s.activation(
                s_E[par][:, hx * HW2:(hx + 1) * HW2], pB[:, hx * HW2:(hx + 1) * HW2], AF.Exp,
                bias=s_negB[bi][:, :]),
               waits=[(tM, st_p2["pb_ms"][h]), (qNB[bi], st_p2["negb_q"][bi])] +
                     ([(vC, st_p2["mask_done"][par])] if hx == 0 else []))
            eh.append(sC.n)
        st_p2["exp_done"][par] = eh[1]
        st_p2["exp_ms"][h] = tuple(eh)

    def p2_mask(qt, h):
        par = h & 1
        (e1, e2) = st_p2["exp_ms"][h]
        at_half = []
        for hx in range(2):
            V(lambda v, h=h, par=par, hx=hx: v.tensor_tensor(
                s_at[par][:, hx * HW2:(hx + 1) * HW2],
                s_E[par][:, hx * HW2:(hx + 1) * HW2],
                MT3[:, h, hx * HW2:(hx + 1) * HW2], ALU.mult),
              waits=[(sC, (e1 if hx == 0 else e2)), (qMT, mt_q)] +
                    ([(tM, st_p2["av_done_t"][par])] if hx == 0 else []))
            at_half.append(vC.n)
        st_p2["at_ms"][h] = tuple(at_half)
        st_p2["mask_done"][par] = vC.n

    def p2_av(qt, h):
        bi = qt & 1
        par = h & 1
        (at1, at2) = st_p2["at_ms"][h]
        pav = p_av[par]
        for jj in range(NJ):
            ktile = HT + qt + J[jj]
            fw = ()
            if jj == 0:
                fw = [(vC, at1), (vC, st_p2["av_free"][par])]
            elif jj == NJ // 2:
                fw = [(vC, at2)]
            T(lambda t, jj=jj, ktile=ktile, h=h, pav=pav: t.matmul(
                pav[:, 0:65],
                s_at[par][:, jj * 128:(jj + 1) * 128],
                V4[:, ktile, h, :],
                start=(jj == 0), stop=(jj == NJ - 1)),
              incs=[(tM, 1)] if jj == NJ - 1 else (),
              waits=fw)
        av_ms = tM.n
        st_p2["av_done_t"][par] = av_ms
        V(lambda v, pav=pav: v.reciprocal(s_rd[:, :], pav[:, 64:65]),
          waits=[(tM, av_ms)] +
                ([(qGT[bi], st_p2["gld"][bi]), (qGD[bi], st_p2["gd_store"][bi])] if h == 0 else []))
        V(lambda v, h=h, bi=bi, pav=pav: v.scalar_tensor_tensor(
            s_gd[bi][:, h * 64:(h + 1) * 64], pav[:, 0:64], s_rd[:, :],
            s_gtt[bi][:, h * 64:(h + 1) * 64], ALU.mult, ALU.mult))
        st_p2["av_free"][par] = vC.n

    V(lambda v: v.memset(s_nbp[:, :], 0.0))
    qG4 = DS("qG4")
    p2_tile_loads(0)
    for grp in range(4):
        p2_passA_grp(0, grp)
    p2_passA_tail(0)
    for qt in range(QT_N):
        bi = qt & 1
        for h in range(H):
            p2_passB_mm(qt, h)
            if h >= 1:
                p2_av(qt, h - 1)
            p2_mask(qt, h)
            if qt + 1 < QT_N:
                if h == 0:
                    p2_tile_loads(qt + 1)
                elif 1 <= h <= 4:
                    p2_passA_grp(qt + 1, h - 1)
                elif h == 5:
                    p2_passA_tail(qt + 1)
        p2_av(qt, H - 1)
        gd_ms = vC.n
        DA(qGD[bi], lambda s, qt=qt, bi=bi: s.dma_start(d_gated.ap()[qt], s_gd[bi][:, :]),
           waits=[(vC, gd_ms)])
        st_p2["gd_store"][bi] = qGD[bi].n

    if tap is not None and tap[0] == "gated":
        barrier()
        DA(qC, lambda s: s.dma_start(tap_t.ap(), d_gated.ap()))
    barrier()
    st2.close()
    st_kv.close()
    if upto == 2:
        _finish(nc, P, _dma_sems)
        ctx.close()
        return nc

    st3 = ExitStack()
    s_xn2 = sb("s_xn2", [128, 8 * OWN], BF16, st3)
    xn23 = s_xn2[:, :].rearrange("p (c t) -> p c t", c=8)
    st3b = ExitStack()
    s_g4 = [sb("s_g40", [128, 4 * D], BF16, st3b), sb("s_g41", [128, 4 * D], BF16, st3b)]
    g43v = [t[:, :].rearrange("p (n d) -> p n d", n=4) for t in s_g4]
    s_wo = sb("s_wo", [128, 8 * 1024], BF16, st3b)
    wo3 = s_wo[:, :].rearrange("p (k o) -> p k o", k=8)
    s_wo32 = sb("s_wo32", [128, 4 * 1024], F32, st3b)
    s_gT = sb("s_gT", [128, 8 * 512], BF16, st3b)
    gT3 = s_gT[:, :].rearrange("p (c t) -> p c t", c=8)
    s_xnb3_ = sb("s_xnb3", [128, 8 * 512], BF16, st3b)
    xnb33 = s_xnb3_[:, :].rearrange("p (c t) -> p c t", c=8)
    s_r1 = sb("s_r1", [128, 8 * 512], F32, st3b)
    r13 = s_r1[:, :].rearrange("p (c t) -> p c t", c=8)
    s_r1b = sb("s_r1b", [128, 8 * 512], BF16, st3b)
    s_row3 = sb("s_row3", [1, 512], F32, st3b)
    r1b3 = s_r1b[:, :].rearrange("p (c t) -> p c t", c=8)
    p_t3 = [ps("p_t30", [128, 128], BF16, st3b), ps("p_t31", [128, 128], BF16, st3b)]
    p_o3 = [ps("p_o30", [128, 512], F32, st3b), ps("p_o31", [128, 512], F32, st3b)]
    p_sr = ps("p_sr", [1, 512], F32, st3b)
    p_bc3 = ps("p_bc3", [128, 512], F32, st3b)

    qXN3 = DS("qXN3")
    qR1S = DS("qR1S")
    qC2 = DS("qC2")
    for c in range(8):
        DW(qC2, lambda s, c=c: s.dma_start(s_g2[:, c:c + 1], g2v.ap()[c * 128:(c + 1) * 128].unsqueeze(1)))
        DW(qC2, lambda s, c=c: s.dma_start(s_s3[:, c:c + 1], s3v.ap()[c * 128:(c + 1) * 128].unsqueeze(1)))
    for c in range(32):
        DW(qC2, lambda s, c=c: s.dma_start(s_s1[:, c:c + 1], s1v.ap()[c * 128:(c + 1) * 128].unsqueeze(1)))
        DW(qC2, lambda s, c=c: s.dma_start(s_s2[:, c:c + 1], s2v.ap()[c * 128:(c + 1) * 128].unsqueeze(1)))
        DW(qC2, lambda s, c=c: s.dma_start(s_sh1[:, c:c + 1], sh1v.ap()[c * 128:(c + 1) * 128].unsqueeze(1)))
        DW(qC2, lambda s, c=c: s.dma_start(s_sh2[:, c:c + 1], sh2v.ap()[c * 128:(c + 1) * 128].unsqueeze(1)))
    qc2_q = qC2.n
    qWOB = DS("qWOB")
    p2_end_t = tM.n
    p2_end_v = vC.n
    p2_end_s = sC.n
    wo_half_v = []
    for wh in range(2):
        DSC(qWOB, lambda s, wh=wh: s.dma_start(
            s_wo32[:, :].rearrange("p (k o) -> p k o", k=4),
            wout.ap().rearrange("(k p) o -> p k o", p=128)[:, wh * 4:(wh + 1) * 4, :]),
            waits=[(tM, p2_end_t)] if wh == 0 else [(vC, vC.n)])
        wob_q = qWOB.n
        V(lambda v, wh=wh: v.tensor_copy(
            s_wo[:, wh * 4096:(wh + 1) * 4096], s_wo32[:, :]),
          waits=[(qWOB, wob_q)])
        wo_half_v.append(vC.n)
    pt_free = [0, 0]
    po_free = [0, 0]
    prev_r1_store = 0
    sr_evac_v = 0
    ti = 0
    tr_ms = [0, 0, 0, 0]
    g4qs = [0, 0, 0, 0]

    def g4_load(blk, fw):
        DA(qG4, lambda s, blk=blk: s.dma_start(
            g43v[blk & 1][:, :, :],
            d_gated.ap().rearrange("n p d -> p n d")[:, blk * 4:(blk + 1) * 4, :]),
           waits=fw)
        g4qs[blk] = qG4.n

    g4_load(0, [(tM, tM.n), (vC, p2_end_v)])
    g4_load(1, [(tM, tM.n), (qG4, qG4.n)])
    for blk in range(4):
        if blk >= 1 and blk + 1 < 4:
            g4_load(blk + 1, [(tM, tr_ms[blk - 1]), (qG4, qG4.n)])
        g43 = g43v[blk & 1]
        g4q = g4qs[blk]
        DA(qXN3, lambda s, blk=blk: s.dma_start(
            xnb33[:, :, :],
            xn_w[:, :, HALO + blk * 512: HALO + (blk + 1) * 512]),
           waits=[(vC, vC.n), (qXN[0], p0["xnst"][0]), (qXN[1], p0["xnst"][1])])
        xnq = qXN3.n
        for ttl in range(4):
            for c in range(8):
                pt = p_t3[ti]
                fw = [(vC, pt_free[ti])] + ([(qG4, g4q)] if (ttl == 0 and c == 0) else [])
                if blk == 0 and ttl == 0 and c == 0:
                    fw += [(vC, p2_end_v), (sC, p2_end_s)]
                T(lambda t, ttl=ttl, c=c, pt=pt, g43=g43: t.transpose(
                    pt[:, :], g43[:, ttl, c * 128:(c + 1) * 128], s_id[:, :]),
                  incs=[(tM, 1)], waits=fw)
                trm = tM.n
                V(lambda v, ttl=ttl, c=c, pt=pt: v.tensor_copy(
                    gT3[:, c, ttl * 128:(ttl + 1) * 128], pt[:, :]),
                  waits=[(tM, trm)])
                pt_free[ti] = vC.n
                ti ^= 1
        gT_ms = vC.n
        tr_ms[blk] = tM.n
        for oc in range(8):
            po = p_o3[oc & 1]
            fw = [(vC, max(po_free[oc & 1], gT_ms))] + \
                 ([(vC, wo_half_v[0])] if (blk == 0 and oc == 0) else [])
            fw4 = [(vC, wo_half_v[1])] if (blk == 0 and oc == 0) else ()
            for kc in range(8):
                T(lambda t, kc=kc, po=po, oc=oc: t.matmul(
                    po[:, :], wo3[:, kc, oc * 128:(oc + 1) * 128], gT3[:, kc, :],
                    start=(kc == 0), stop=(kc == 7)),
                  incs=[(tM, 1)] if kc == 7 else (),
                  waits=fw if kc == 0 else (fw4 if kc == 4 else ()))
            omm = tM.n
            V(lambda v, oc=oc, po=po: v.tensor_tensor(
                r13[:, oc, :], po[:, :], xnb33[:, oc, :], ALU.add),
              waits=[(tM, omm), (qXN3, xnq)] + ([(qR1S, prev_r1_store)] if oc == 0 else []))
            po_free[oc & 1] = vC.n
        r1_ms = vC.n
        DA(qR1S, lambda s, blk=blk: s.dma_start(
            d_r1.ap().rearrange("c p t -> p c t")[:, :, blk * 512:(blk + 1) * 512],
            r13[:, :, :]),
           waits=[(vC, r1_ms)])
        prev_r1_store = qR1S.n
        V(lambda v: v.tensor_copy(s_r1b[:, :], s_r1[:, :]))
        V(lambda v: v.tensor_mul(s_r1b[:, :], s_r1b[:, :], s_r1b[:, :]))
        sq_ms = vC.n
        for c in range(8):
            T(lambda t, c=c: t.matmul(p_sr[:, :], s_onescb[:, :], r1b3[:, c, :],
                                      start=(c == 0), stop=(c == 7)),
              incs=[(tM, 1)] if c == 7 else (),
              waits=[(vC, max(sq_ms, sr_evac_v))] if c == 0 else ())
        srm = tM.n
        V(lambda v: v.tensor_copy(s_row3[:, :], p_sr[:, :]), waits=[(tM, srm)])
        sr_evac_v = vC.n
        V(lambda v: v.tensor_scalar_mul(s_row3[:, :], s_row3[:, :], 1.0 / D))
        vr = vC.n
        SE(lambda s: s.activation(s_row3[:, :], s_row3[:, :], AF.Ln, bias=s_cv[0:1, 1:2]), waits=[(vC, vr)])
        SE(lambda s: s.activation(s_row3[:, :], s_row3[:, :], AF.Exp, scale=s_cv[0:1, 2:3]))
        rr_ms = sC.n
        T(lambda t: t.matmul(p_bc3[:, :], s_onesf[:, :], s_row3[:, :], start=True, stop=True),
          incs=[(tM, 1)], waits=[(sC, rr_ms), (vC, vC.n)])
        bc_ms = tM.n
        for c in range(8):
            V(lambda v, c=c, blk=blk: v.scalar_tensor_tensor(
                xn23[:, c, blk * 512:(blk + 1) * 512], r13[:, c, :], s_g2[:, c:c + 1],
                p_bc3[:, :], ALU.mult, ALU.mult),
              waits=([(tM, bc_ms)] + ([(qC2, qc2_q)] if blk == 0 else [])) if c == 0 else ())

    barrier()
    st3b.close()
    if upto == 3:
        _finish(nc, P, _dma_sems)
        st3.close()
        ctx.close()
        return nc

    st4 = ExitStack()
    s_hatt = sb("s_hatt", [128, 22 * OWN], BF16, st4)
    ha3 = s_hatt[:, :].rearrange("p (c t) -> p c t", c=22)
    st4w = ExitStack()
    s_wga = sb("s_wga", [128, 8 * FFA], BF16, st4w)
    wga3 = s_wga[:, :].rearrange("p (k o) -> p k o", k=8)
    p_f = [ps("p_f0", [128, 2048], F32, st4), ps("p_f1", [128, 2048], F32, st4)]
    pf_free_v = [0, 0]
    pf_free_s = [0, 0]

    wfg_v = wfg.ap().rearrange("(k p) o -> p k o", p=128)
    wfu_v = wfu.ap().rearrange("(k p) o -> p k o", p=128)
    qR14 = DS("qR14")
    qR2S = DS("qR2S")
    QB4 = [(0, 6), (6, 12), (12, 17), (17, 22)]
    qW4 = DS("qW4")
    wgaq_q = []
    for (a, b) in QB4:
        DW(qW4, lambda g, a=a, b=b: g.dma_start(
            wga3[:, :, a * 128:b * 128], wfg_v[:, :, a * 128:b * 128]),
           waits=([(tM, tM.n)] if a == 0 else []) + [(qW4, qW4.n)])
        wgaq_q.append(qW4.n)
    def _q4(oc):
        for qi, (a, b) in enumerate(QB4):
            if oc < b:
                return qi
    wfg_oc_t = {}
    for oc in range(22):
        pf = p_f[oc & 1]
        fw = [(sC, pf_free_s[oc & 1]), (qW4, wgaq_q[_q4(oc)])]
        for kc in range(8):
            for b in range(4):
                T(lambda t, oc=oc, kc=kc, b=b, pf=pf: t.matmul(
                    pf[:, b * 512:(b + 1) * 512],
                    wga3[:, kc, oc * 128:(oc + 1) * 128],
                    xn23[:, kc, b * 512:(b + 1) * 512],
                    start=(kc == 0), stop=(kc == 7)),
                  incs=[(tM, 1)] if (kc == 7 and b == 3) else (),
                  waits=fw if (kc == 0 and b == 0) else ())
        fm = tM.n
        wfg_oc_t[oc] = fm
        SE(lambda s, oc=oc, pf=pf: s.activation(ha3[:, oc, :], pf[:, :], AF.Silu),
           waits=[(tM, fm)])
        pf_free_s[oc & 1] = sC.n
    g_done_t = tM.n
    wuq_q = []
    for (a, b) in QB4:
        DW(qW4, lambda g, a=a, b=b: g.dma_start(
            wga3[:, :, a * 128:b * 128], wfu_v[:, :, a * 128:b * 128]),
           waits=[(tM, wfg_oc_t[b - 1]), (qW4, qW4.n)])
        wuq_q.append(qW4.n)
    for oc in range(22):
        pf = p_f[oc & 1]
        fw = [(vC, pf_free_v[oc & 1]), (sC, sC.n), (qW4, wuq_q[_q4(oc)])]
        for kc in range(8):
            for b in range(4):
                T(lambda t, oc=oc, kc=kc, b=b, pf=pf: t.matmul(
                    pf[:, b * 512:(b + 1) * 512],
                    wga3[:, kc, oc * 128:(oc + 1) * 128],
                    xn23[:, kc, b * 512:(b + 1) * 512],
                    start=(kc == 0), stop=(kc == 7)),
                  incs=[(tM, 1)] if (kc == 7 and b == 3) else (),
                  waits=fw if (kc == 0 and b == 0) else ())
        fm = tM.n
        V(lambda v, oc=oc, pf=pf: v.tensor_tensor(ha3[:, oc, :], ha3[:, oc, :], pf[:, :], ALU.mult),
          waits=[(tM, fm)])
        pf_free_v[oc & 1] = vC.n
    u_done_t = tM.n
    u_done_v = vC.n
    barrier()
    st4w.close()

    s_wd = sb("s_wd", [128, 22 * D], BF16, st4)
    wd3 = s_wd[:, :].rearrange("p (k o) -> p k o", k=22)
    s_r1s = sb("s_r1s", [128, OWN], F32, st4)
    s_r2s = sb("s_r2s", [128, OWN], F32, st4)
    wfd_v = wfd.ap().rearrange("(k p) o -> p k o", p=128)
    wdq_q = []
    for q in range(4):
        DW(qW4, lambda g, q=q: g.dma_start(
            wd3[:, :, q * 256:(q + 1) * 256], wfd_v[:, :, q * 256:(q + 1) * 256]),
           waits=([(tM, u_done_t)] if q == 0 else []) + [(qW4, qW4.n)])
        wdq_q.append(qW4.n)
    for oc in range(8):
        pf = p_f[oc & 1]
        DA(qR14, lambda s, oc=oc: s.dma_start(s_r1s[:, :], d_r1.ap()[oc]),
           waits=[(vC, vC.n)])
        r1q = qR14.n
        fw = [(vC, max(pf_free_v[oc & 1], u_done_v)), (qW4, wdq_q[oc // 2])]
        for kc in range(22):
            for b in range(4):
                T(lambda t, oc=oc, kc=kc, b=b, pf=pf: t.matmul(
                    pf[:, b * 512:(b + 1) * 512],
                    wd3[:, kc, oc * 128:(oc + 1) * 128],
                    ha3[:, kc, b * 512:(b + 1) * 512],
                    start=(kc == 0), stop=(kc == 21)),
                  incs=[(tM, 1)] if (kc == 21 and b == 3) else (),
                  waits=fw if (kc == 0 and b == 0) else ())
        fm = tM.n
        V(lambda v, pf=pf: v.tensor_tensor(s_r2s[:, :], pf[:, :], s_r1s[:, :], ALU.add),
          waits=[(tM, fm), (qR14, r1q), (qR2S, qR2S.n)])
        pf_free_v[oc & 1] = vC.n
        vv = vC.n
        DA(qR2S, lambda s, oc=oc: s.dma_start(d_r2.ap()[oc], s_r2s[:, :]), waits=[(vC, vv)])

    barrier()
    st4.close()
    st3.close()
    if upto == 4:
        _finish(nc, P, _dma_sems)
        ctx.close()
        return nc

    st5 = ExitStack()
    s_ha5 = sb("s_ha5", [128, 32 * 1024], BF16, st5)
    h53 = s_ha5[:, :].rearrange("p (c t) -> p c t", c=32)
    s_hb5 = sb("s_hb5", [128, 32 * 1024], BF16, st5)
    hb53 = s_hb5[:, :].rearrange("p (c t) -> p c t", c=32)
    xh3 = s_hb5[:, 0:8 * 1024].rearrange("p (c t) -> p c t", c=8)
    s_hs = [sb("s_hs0", [128, 32 * 256], BF16, st5), sb("s_hs1", [128, 32 * 256], BF16, st5)]
    s_rr = sb("s_rr", [128, 1024], F32, st5)
    s_xx = sb("s_xx", [128, 1024], F32, st5)
    s_oo = sb("s_oo", [128, 1024], F32, st5)
    p_5 = [ps("p_50", [128, 1024], F32, st5), ps("p_51", [128, 1024], F32, st5),
           ps("p_52", [128, 1024], F32, st5), ps("p_53", [128, 1024], F32, st5)]

    w1_v = w1T.ap().rearrange("(k p) o -> p k o", p=128)
    w2_v = w2T.ap().rearrange("(k p) o -> p k o", p=128)
    w3_v = w3T.ap().rearrange("(k p) o -> p k o", p=128)
    xTv2 = xT.ap().rearrange("(c p) t -> p c t", p=128)
    out_v = out_t.ap().rearrange("(c p) t -> p c t", p=128)

    wS5 = [DS("wS50"), DS("wS51")]
    qXH = DS("qXH")
    qRX = DS("qRX")
    qOUT = DS("qOUT")
    p5_free_v = [0] * 4
    p5_free_s = [0] * 4
    hs_free_t = [0, 0]
    state = {"fm": 0}

    def gemm_slab(wv, n_kc, src3, n_oc, evac, gate=()):
        gate = list(gate)
        for og in range(n_oc // 2):
            sl3 = s_hs[og & 1][:, 0:n_kc * 256].rearrange("p (k o) -> p k o", k=n_kc)
            DW(wS5[og & 1], lambda g, og=og, wv=wv, sl3=sl3: g.dma_start(
                sl3[:, :, :], wv[:, :, og * 256:(og + 1) * 256]),
               waits=[(tM, hs_free_t[og & 1])])
            slq = wS5[og & 1].n
            for o2 in range(2):
                oc = og * 2 + o2
                pp = p_5[oc & 3]
                fw = [(wS5[og & 1], slq), (vC, p5_free_v[oc & 3]), (sC, p5_free_s[oc & 3])]
                if og == 0 and o2 == 0:
                    fw = fw + gate
                for kc in range(n_kc):
                    for b in range(2):
                        T(lambda t, o2=o2, kc=kc, b=b, pp=pp, src3=src3, sl3=sl3, n_kc=n_kc: t.matmul(
                            pp[:, b * 512:(b + 1) * 512],
                            sl3[:, kc, o2 * 128:(o2 + 1) * 128],
                            src3[:, kc, b * 512:(b + 1) * 512],
                            start=(kc == 0), stop=(kc == n_kc - 1)),
                          incs=[(tM, 1)] if (kc == n_kc - 1 and b == 1) else (),
                          waits=fw if (kc == 0 and b == 0) else ())
                state["fm"] = tM.n
                if o2 == 1:
                    hs_free_t[og & 1] = tM.n
                evac(oc, pp, tM.n)

    for hf in range(2):
        DA(qXH, lambda s, hf=hf: s.dma_start(
            xh3[:, :, :],
            xn_w[:, :, HALO + hf * 1024: HALO + (hf + 1) * 1024]),
           waits=[(tM, tM.n)])
        xhq = qXH.n
        P.q["tensor"].append(lambda t, _v=xhq: t.wait_ge(qXH.h, _v))

        def ev_w1(oc, pp, fm):
            SE(lambda s, oc=oc, pp=pp: s.activation(
                h53[:, oc, :], pp[:, :], AF.Silu, scale=s_s1[:, oc:oc + 1]),
               waits=[(tM, fm)] + ([(qC2, qc2_q)] if oc == 0 else []))
            p5_free_s[oc & 3] = sC.n

        def ev_w2(oc, pp, fm):
            V(lambda v, oc=oc, pp=pp: v.scalar_tensor_tensor(
                h53[:, oc, :], pp[:, :], s_s2[:, oc:oc + 1], h53[:, oc, :],
                ALU.mult, ALU.mult),
              waits=[(tM, fm), (sC, sC.n)])
            p5_free_v[oc & 3] = vC.n

        gemm_slab(w1_v, 8, xh3, 32, ev_w1)
        gemm_slab(w2_v, 8, xh3, 32, ev_w2)

        def mk_ev_h(dst3, ssc):
            def ev(oc, pp, fm):
                SE(lambda s, oc=oc, pp=pp: s.activation(
                    dst3[:, oc, :], pp[:, :], AF.Gelu, scale=ssc[:, oc:oc + 1]),
                   waits=[(tM, fm)])
                p5_free_s[oc & 3] = sC.n
            return ev

        gemm_slab(h1T.ap().rearrange("(k p) o -> p k o", p=128), 32, h53, 32,
                  mk_ev_h(hb53, s_sh1), gate=[(vC, vC.n), (sC, sC.n)])
        gemm_slab(h2T.ap().rearrange("(k p) o -> p k o", p=128), 32, hb53, 32,
                  mk_ev_h(h53, s_sh2), gate=[(sC, sC.n)])

        def mk_ev_w3(hf):
            def ev(oc, pp, fm):
                DA(qRX, lambda s, oc=oc, hf=hf: s.dma_start(
                    s_rr[:, :], d_r2.ap()[oc][:, hf * 1024:(hf + 1) * 1024]),
                   waits=[(vC, vC.n)])
                DA(qRX, lambda s, oc=oc, hf=hf: s.dma_start(
                    s_xx[:, :], xTv2[:, oc, HALO + hf * 1024: HALO + (hf + 1) * 1024]),
                   waits=[(vC, vC.n)])
                xq5 = qRX.n
                V(lambda v, oc=oc, pp=pp: v.scalar_tensor_tensor(
                    s_oo[:, :], pp[:, :], s_s3[:, oc:oc + 1], s_rr[:, :], ALU.mult, ALU.add),
                  waits=[(tM, fm), (qRX, xq5), (qOUT, qOUT.n)])
                V(lambda v: v.tensor_add(s_oo[:, :], s_oo[:, :], s_xx[:, :]))
                p5_free_v[oc & 3] = vC.n
                vv = vC.n
                DA(qOUT, lambda s, oc=oc, hf=hf: s.dma_start(
                    out_v[:, oc, hf * 1024:(hf + 1) * 1024], s_oo[:, :]),
                   waits=[(vC, vv)])
            return ev

        gemm_slab(w3_v, 32, h53, 8, mk_ev_w3(hf), gate=[(sC, sC.n)])

    _finish(nc, P, _dma_sems)
    st5.close()
    ctx.close()
    return nc


_NC_CACHE = {}


def kernel(**inputs):
    x = np.asarray(inputs["L0_attn_x"], dtype=np.float32)
    offsets = np.asarray(inputs["L0_attn_offsets"]).astype(np.int64)
    offs = sorted(int(o) for o in offsets.tolist())
    OMAX = max(offs)
    HALO = ((OMAX + 127) // 128) * 128

    key = (tuple(offs),)
    if key not in _NC_CACHE:
        _NC_CACHE[key] = build_nc(offsets)
    nc = _NC_CACHE[key]

    Bsz, N, Dm = x.shape
    per_b = NCORES // Bsz
    assert N // per_b == OWN and Dm == D

    def cvt(a):
        return np.ascontiguousarray(np.asarray(a, dtype=np.float32))

    shared = {
        "wqkv": cvt(inputs["L0_attn_w_qkv"]),
        "wgate": cvt(inputs["L0_attn_w_gate"]),
        "wout": cvt(inputs["L0_attn_w_out"]),
        "wfg": cvt(inputs["L0_attn_w_ffn_gate"]),
        "wfu": cvt(inputs["L0_attn_w_ffn_up"]),
        "wfd": cvt(inputs["L0_attn_w_ffn_down"]),
        "w1T": cvt(np.asarray(inputs["L0_ffn_w1_w"], dtype=np.float32).T),
        "w2T": cvt(np.asarray(inputs["L0_ffn_w2_w"], dtype=np.float32).T),
        "h1T": cvt(np.asarray(inputs["L0_ffn_h1_w"], dtype=np.float32).T),
        "h2T": cvt(np.asarray(inputs["L0_ffn_h2_w"], dtype=np.float32).T),
        "w3T": cvt(np.asarray(inputs["L0_ffn_w3_w"], dtype=np.float32).T),
        "g1v": cvt(inputs["L0_attn_norm1_scale"]),
        "g2v": cvt(inputs["L0_attn_norm2_scale"]),
        "bgv": cvt(inputs["L0_attn_b_gate"]),
        "s1v": cvt(inputs["L0_ffn_w1_s"]),
        "s2v": cvt(inputs["L0_ffn_w2_s"]),
        "sh1v": cvt(inputs["L0_ffn_h1_s"]),
        "sh2v": cvt(inputs["L0_ffn_h2_s"]),
        "s3v": cvt(inputs["L0_ffn_w3_s"]),
        "posb": cvt(inputs["L0_attn_pos_bias"]),
        "c_iota": np.arange(128, dtype=np.float32).reshape(128, 1),
        "c_onesf": np.ones((1, 128), dtype=np.float32),
        "c_onesb": np.ones((128, 128), dtype=BF),
        "c_onescb": np.ones((128, 1), dtype=BF),
        "c_id": np.eye(128, dtype=BF),
        "c_cv": np.array([[1.0e-5, 1.0e-6, -0.5, 0.0]], dtype=np.float32),
    }

    in_maps = []
    for c in range(NCORES):
        b = c // per_b
        s0 = (c % per_b) * OWN
        lo = s0 - HALO
        if lo >= 0:
            seg = x[b, lo:s0 + OWN, :]
        else:
            seg = np.concatenate(
                [np.zeros((-lo, D), dtype=np.float32), x[b, 0:s0 + OWN, :]], axis=0)
        m = dict(shared)
        m["xT"] = np.ascontiguousarray(seg.T)
        m["tbase"] = np.full((128, 1), float(s0), dtype=np.float32)
        in_maps.append(m)

    res = run_bass_kernel_spmd(nc, in_maps, core_ids=list(range(NCORES)))

    out = np.empty((Bsz, N, D), dtype=np.float32)
    for c in range(NCORES):
        b = c // per_b
        s0 = (c % per_b) * OWN
        out[b, s0:s0 + OWN, :] = np.asarray(res.results[c]["out"]).T
    return out


# version 15 (final)
# speedup vs baseline: 1.0984x; 1.0098x over previous
from contextlib import ExitStack

import numpy as np
import ml_dtypes
import concourse.bass as bass
import concourse.mybir as mybir
from concourse.bass_utils import run_bass_kernel_spmd

F32 = mybir.dt.float32
BF16 = mybir.dt.bfloat16
AF = mybir.ActivationFunctionType
ALU = mybir.AluOpType
AX = mybir.AxisListType
BF = ml_dtypes.bfloat16

D = 1024
H = 16
NCORES = 8
OWN = 2048
FFA = 2816
FFH = 4096
QT_N = OWN // 128


def _offset_runs(positions):
    runs = []
    i = 0
    pos = list(positions)
    n = len(pos)
    while i < n:
        if i + 1 < n:
            step = pos[i + 1] - pos[i]
            j = i + 1
            while j + 1 < n and pos[j + 1] - pos[j] == step:
                j += 1
            runs.append((i, step, j - i + 1))
            i = j + 1
        else:
            runs.append((i, 1, 1))
            i += 1
    return runs


class Ctr:
    def __init__(self, h):
        self.h = h
        self.n = 0


class Prog:
    def __init__(self):
        self.q = {"tensor": [], "vector": [], "scalar": [], "gpsimd": [], "sync": []}

    def emit(self, eng, fn, incs=(), waits=()):
        lst = self.q[eng]
        for (c, v) in waits:
            if v > 0:
                lst.append(lambda e, _h=c.h, _v=v: e.wait_ge(_h, _v))
        ivals = []
        for (c, k) in incs:
            c.n += k
            ivals.append((c.h, k))

        def run(e, _fn=fn, _iv=tuple(ivals)):
            inst = _fn(e)
            for (_h, _k) in _iv:
                inst = inst.then_inc(_h, _k)

        lst.append(run)


def _finish(nc, P, dma_sems):
    for c_ in dma_sems.values():
        if c_.n > 0:
            P.emit("sync", lambda s, _h=c_.h, _v=c_.n: s.wait_ge(_h, _v))
    with nc.Block() as blk:
        blk.tensor(lambda t: [f(t) for f in P.q["tensor"]])
        blk.vector(lambda v: [f(v) for f in P.q["vector"]])
        blk.scalar(lambda s: [f(s) for f in P.q["scalar"]])
        blk.gpsimd(lambda g: [f(g) for f in P.q["gpsimd"]])
        blk.sync(lambda s: [f(s) for f in P.q["sync"]])


def build_nc(offsets, tap=None, upto=99, p2sub=99):
    offs = sorted(int(o) for o in np.asarray(offsets).tolist())
    OMAX = max(offs)
    HALO = ((OMAX + 127) // 128) * 128
    TOT = OWN + HALO
    NB = TOT // 512
    NT = TOT // 128
    HT = HALO // 128

    Jset = set()
    for o in offs:
        Jset.add((-o) // 128)
        Jset.add((127 - o) // 128)
    J = sorted(Jset, reverse=True)
    NJ = len(J)
    assert min(J) >= -HT and NJ % 4 == 0
    GL = ((OMAX + 256 + 511) // 512) * 512

    nc = bass.Bass("TRN2", target_bir_lowering=False)

    def din(name, shape, dt=F32):
        return nc.dram_tensor(name, shape, dt, kind="ExternalInput")

    xT = din("xT", [D, TOT])
    wqkv = din("wqkv", [D, 3 * D])
    wgate = din("wgate", [D, D])
    wout = din("wout", [D, D])
    wfg = din("wfg", [D, FFA])
    wfu = din("wfu", [D, FFA])
    wfd = din("wfd", [FFA, D])
    w1T = din("w1T", [D, FFH])
    w2T = din("w2T", [D, FFH])
    h1T = din("h1T", [FFH, FFH])
    h2T = din("h2T", [FFH, FFH])
    w3T = din("w3T", [FFH, D])
    g1v = din("g1v", [D])
    g2v = din("g2v", [D])
    bgv = din("bgv", [D])
    s1v = din("s1v", [FFH])
    s2v = din("s2v", [FFH])
    sh1v = din("sh1v", [FFH])
    sh2v = din("sh2v", [FFH])
    s3v = din("s3v", [D])
    posb = din("posb", [len(offs), H])
    tbase = din("tbase", [128, 1])
    c_iota = din("c_iota", [128, 1])
    c_onesf = din("c_onesf", [1, 128])
    c_onesb = din("c_onesb", [128, 128], BF16)
    c_onescb = din("c_onescb", [128, 1], BF16)
    c_id = din("c_id", [128, 128], BF16)
    c_cv = din("c_cv", [1, 4])

    out_t = nc.dram_tensor("out", [D, OWN], F32, kind="ExternalOutput")
    tap_t = None
    if tap is not None:
        tap_t = nc.dram_tensor("tapout", list(tap[1]), tap[2], kind="ExternalOutput")

    d_xn = nc.dram_tensor("d_xn", [8, 128, TOT], BF16, kind="Internal")
    d_q = nc.dram_tensor("d_q", [8, 128, OWN], BF16, kind="Internal")
    d_gate = nc.dram_tensor("d_gate", [QT_N, 128, D], BF16, kind="Internal")
    d_gated = nc.dram_tensor("d_gated", [QT_N, 128, D], BF16, kind="Internal")
    d_r1 = nc.dram_tensor("d_r1", [8, 128, OWN], F32, kind="Internal")
    d_r2 = nc.dram_tensor("d_r2", [8, 128, OWN], F32, kind="Internal")
    d_gtab = nc.dram_tensor("d_gtab", [H, GL], BF16, kind="Internal")
    d_grep = nc.dram_tensor("d_grep", [H, 128, GL], BF16, kind="Internal")
    d_nb = nc.dram_tensor("d_nb", [2, H, 128], BF16, kind="Internal")
    d_negb = nc.dram_tensor("d_negb", [2], F32, kind="Internal")

    P = Prog()
    ctx = ExitStack()

    def sem(name):
        return Ctr(ctx.enter_context(nc.semaphore(name)))

    tM = sem("tM")
    vC = sem("vC")
    sC = sem("sC")
    _dma_sems = {}

    def DS(name):
        if name not in _dma_sems:
            _dma_sems[name] = sem(name)
        return _dma_sems[name]

    def sb(name, shape, dt, stack=None, side=None):
        if side is None:
            return (stack or ctx).enter_context(nc.sbuf_tensor(name, shape, dt))
        return (stack or ctx).enter_context(nc.sbuf_tensor(name, shape, dt, side=side))

    def ps(name, shape, dt=F32, stack=None):
        return (stack or ctx).enter_context(nc.psum_tensor(name, shape, dt))

    def T(fn, incs=(), waits=()):
        P.emit("tensor", fn, incs, waits)

    def V(fn, waits=()):
        P.emit("vector", fn, incs=[(vC, 1)], waits=[(vC, vC.n)] + list(waits))

    def SE(fn, waits=()):
        P.emit("scalar", fn, incs=[(sC, 1)], waits=[(sC, sC.n)] + list(waits))

    def DW(sm, fn, waits=()):
        P.emit("gpsimd", fn, incs=[(sm, 16)], waits=waits)

    def DA(sm, fn, waits=()):
        P.emit("sync", fn, incs=[(sm, 16)], waits=waits)

    def DSC(sm, fn, waits=()):
        P.emit("scalar", fn, incs=[(sm, 16)], waits=waits)

    def barrier():
        vals = [(tM, tM.n), (vC, vC.n), (sC, sC.n)]
        vals += [(c, c.n) for c in _dma_sems.values()]
        for eng in ("tensor", "vector", "scalar", "gpsimd", "sync"):
            for (c, v) in vals:
                if v > 0:
                    P.emit(eng, lambda e, _h=c.h, _v=v: e.wait_ge(_h, _v))

    s_onesf = sb("s_onesf", [1, 128], F32)
    s_onesb = sb("s_onesb", [128, 128], BF16)
    s_onescb = sb("s_onescb", [128, 1], BF16)
    s_id = sb("s_id", [128, 128], BF16)
    s_iota = sb("s_iota", [128, 1], F32)
    s_tb = sb("s_tb", [128, 1], F32)
    s_tvec = sb("s_tvec", [128, 1], F32)
    s_g1 = sb("s_g1", [128, 8], F32)
    s_g2 = sb("s_g2", [128, 8], F32)
    s_s1 = sb("s_s1", [128, 32], F32)
    s_s2 = sb("s_s2", [128, 32], F32)
    s_sh1 = sb("s_sh1", [128, 32], F32)
    s_sh2 = sb("s_sh2", [128, 32], F32)
    s_s3 = sb("s_s3", [128, 8], F32)
    s_bgbc = sb("s_bgbc", [128, D], F32)
    s_cv = sb("s_cv", [1, 4], F32)

    qC = DS("qC")
    DA(qC, lambda s: s.dma_start(s_onesf[:, :], c_onesf.ap()))
    DA(qC, lambda s: s.dma_start(s_onesb[:, :], c_onesb.ap()))
    DA(qC, lambda s: s.dma_start(s_onescb[:, :], c_onescb.ap()))
    DA(qC, lambda s: s.dma_start(s_id[:, :], c_id.ap()))
    DA(qC, lambda s: s.dma_start(s_iota[:, :], c_iota.ap()))
    DA(qC, lambda s: s.dma_start(s_tb[:, :], tbase.ap()))
    for c in range(8):
        DA(qC, lambda s, c=c: s.dma_start(s_g1[:, c:c + 1], g1v.ap()[c * 128:(c + 1) * 128].unsqueeze(1)))
    DA(qC, lambda s: s.dma_start(s_cv[:, :], c_cv.ap()))
    DA(qC, lambda s: s.dma_start(s_bgbc[:, :], bass.AP(bgv, 0, [[0, 128], [1, D]])))
    consts_q = qC.n

    V(lambda v: v.tensor_add(s_tvec[:, :], s_iota[:, :], s_tb[:, :]),
      waits=[(qC, consts_q)])

    st0 = ExitStack()
    s_xa = sb("s_xa", [128, 8 * TOT], BF16, st0, side="right")
    xa3 = s_xa[:, :].rearrange("p (c t) -> p c t", c=8)
    qG = DS("qG")
    stg = ExitStack()
    s_gpb = sb("s_gpb", [H, GL], F32, stg, side="right")
    s_gme = sb("s_gme", [H, GL], BF16, stg, side="right")
    V(lambda v: v.memset(s_gpb[:, :], -1.0e30))
    gpb_ms = vC.n
    for (i0, step, cnt) in _offset_runs([o + 127 for o in offs]):
        p0 = offs[i0] + 127

        def scat(s, i0=i0, step=step, cnt=cnt, p0=p0):
            src = posb.ap()[i0:i0 + cnt, :].transpose([1, 0])
            dst = s_gpb[:, p0:p0 + step * cnt:step] if cnt > 1 else s_gpb[:, p0:p0 + 1]
            return s.dma_start(dst, src, allow_slow_non_contiguous=True)
        DA(qG, scat, waits=[(vC, gpb_ms), (qC, consts_q)])
    scat_q = qG.n
    SE(lambda s: s.activation(s_gme[:, :], s_gpb[:, :], AF.Exp),
       waits=[(qG, scat_q)])
    gme_ms = sC.n
    DA(qG, lambda s: s.dma_start(d_gtab.ap(), s_gme[:, :]), waits=[(sC, gme_ms)])
    gtab_q = qG.n
    for h in range(H):
        DA(qG, lambda s, h=h: s.dma_start(
            d_grep.ap()[h], bass.AP(d_gtab, h * GL, [[0, 128], [1, GL]])),
           waits=[(qG, gtab_q)])
    grep_q = qG.n
    g_read_done = (gme_ms, gtab_q)
    stg.close()

    st0b = ExitStack()
    s_row = sb("s_row", [1, 512 * 4], F32, st0b, side="right")
    s_x32 = [sb("s_x320", [128, 8 * 512], F32, st0b, side="right"),
             sb("s_x321", [128, 8 * 512], F32, st0b, side="right")]
    s_xb = [sb("s_xb0", [128, 8 * 512], BF16, st0b, side="right"),
            sb("s_xb1", [128, 8 * 512], BF16, st0b, side="right")]
    s_xnb = [sb("s_xnb0", [128, 8 * 512], BF16, st0b, side="right"),
             sb("s_xnb1", [128, 8 * 512], BF16, st0b, side="right")]
    p_s1 = [ps("p_s1a", [1, 512], F32, st0b), ps("p_s1b", [1, 512], F32, st0b)]
    p_s2 = [ps("p_s2a", [1, 512], F32, st0b), ps("p_s2b", [1, 512], F32, st0b)]
    p_b1 = ps("p_b1", [128, 512], F32, st0b)
    p_b2 = ps("p_b2", [128, 512], F32, st0b)
    p_b3 = ps("p_b3", [128, 512], F32, st0b)

    xTv = xT.ap().rearrange("(c p) t -> p c t", p=128)
    xn_w = d_xn.ap().rearrange("c p t -> p c t")

    def row(i):
        return s_row[:, i * 512:(i + 1) * 512]

    qX0 = [DS("qX00"), DS("qX01")]
    wXB = [DS("wXB0"), DS("wXB1")]
    qXN = [DS("qXN0"), DS("qXN1")]
    p0 = {
        "apl_v": [0, 0], "sq_v": [0, 0], "stat_tm": [0, 0], "statevac_v": [0, 0],
        "xq": [0, 0], "wq": [0, 0], "xnst": [0, 0], "bc_ms": 0, "stat_ms": [0, 0],
    }

    def p0_load(blk):
        p = blk & 1
        cslice = slice(blk * 512, (blk + 1) * 512)
        gw = [(sC, g_read_done[0]), (qG, g_read_done[1])] if blk == 0 else []
        DA(qX0[p], lambda s, cslice=cslice, p=p: s.dma_start(
            s_x32[p][:, :].rearrange("p (c t) -> p c t", c=8), xTv[:, :, cslice]),
           waits=[(vC, p0["apl_v"][p])] + gw)
        p0["xq"][p] = qX0[p].n
        DW(wXB[p], lambda g, cslice=cslice, p=p: g.dma_start(
            s_xb[p][:, :].rearrange("p (c t) -> p c t", c=8), xTv[:, :, cslice]),
           waits=[(vC, p0["sq_v"][p]), (tM, p0["stat_tm"][p])] + gw)
        p0["wq"][p] = wXB[p].n

    def p0_stats(blk):
        p = blk & 1
        xb3 = s_xb[p][:, :].rearrange("p (c t) -> p c t", c=8)
        sq3 = s_xnb[p][:, :].rearrange("p (c t) -> p c t", c=8)
        V(lambda v, p=p: v.tensor_mul(s_xnb[p][:, :], s_xb[p][:, :], s_xb[p][:, :]),
          waits=[(wXB[p], p0["wq"][p]), (qXN[p], p0["xnst"][p])])
        sq_ms = vC.n
        p0["sq_v"][p] = sq_ms
        for c in range(8):
            T(lambda t, c=c, p=p, xb3=xb3: t.matmul(p_s1[p][:, :], s_onescb[:, :], xb3[:, c, :],
                                                    start=(c == 0), stop=(c == 7)),
              waits=[(wXB[p], p0["wq"][p]), (vC, p0["statevac_v"][p])] if c == 0 else ())
        for c in range(8):
            T(lambda t, c=c, p=p, sq3=sq3: t.matmul(p_s2[p][:, :], s_onescb[:, :], sq3[:, c, :],
                                                    start=(c == 0), stop=(c == 7)),
              incs=[(tM, 1)] if c == 7 else (),
              waits=[(vC, sq_ms)] if c == 0 else ())
        p0["stat_ms"][p] = tM.n
        p0["stat_tm"][p] = tM.n

    def p0_rows(blk):
        p = blk & 1
        V(lambda v, p=p: v.tensor_copy(row(0), p_s1[p][:, :]),
          waits=[(tM, p0["stat_ms"][p]), (tM, p0["bc_ms"])])
        V(lambda v, p=p: v.tensor_copy(row(1), p_s2[p][:, :]))
        p0["statevac_v"][p] = vC.n
        V(lambda v: v.tensor_scalar_mul(row(0), row(0), 1.0 / D))
        V(lambda v: v.tensor_mul(row(2), row(0), row(0)))
        V(lambda v: v.tensor_scalar_mul(row(1), row(1), 1.0 / D))
        V(lambda v: v.tensor_sub(row(1), row(1), row(2)))
        vvar = vC.n
        SE(lambda s: s.activation(row(2), row(1), AF.Ln, bias=s_cv[0:1, 0:1]), waits=[(vC, vvar)])
        SE(lambda s: s.activation(row(2), row(2), AF.Exp, scale=s_cv[0:1, 2:3]))
        se_rstd = sC.n
        V(lambda v: v.tensor_scalar_add(row(3), row(1), 1.0e-5), waits=[(sC, se_rstd)])
        V(lambda v: v.reciprocal(row(3), row(3)))
        V(lambda v: v.tensor_scalar(row(3), row(3), -1.0e-5, 1.0 + 1.0e-6, ALU.mult, ALU.add))
        vq_ = vC.n
        SE(lambda s: s.activation(row(3), row(3), AF.Ln), waits=[(vC, vq_)])
        SE(lambda s: s.activation(row(3), row(3), AF.Exp, scale=s_cv[0:1, 2:3]))
        se_rr = sC.n
        V(lambda v: v.tensor_scalar_mul(row(0), row(0), -1.0))
        V(lambda v: v.tensor_mul(row(3), row(3), row(2)), waits=[(sC, se_rr)])
        vsm = vC.n
        T(lambda t: t.matmul(p_b1[:, :], s_onesf[:, :], row(0), start=True, stop=True),
          waits=[(vC, vsm), (sC, se_rr)])
        T(lambda t: t.matmul(p_b2[:, :], s_onesf[:, :], row(2), start=True, stop=True))
        T(lambda t: t.matmul(p_b3[:, :], s_onesf[:, :], row(3), start=True, stop=True),
          incs=[(tM, 1)])
        p0["bc_ms"] = tM.n

    def p0_apply(blk):
        p = blk & 1
        cslice = slice(blk * 512, (blk + 1) * 512)
        x323 = s_x32[p][:, :].rearrange("p (c t) -> p c t", c=8)
        xnb3 = s_xnb[p][:, :].rearrange("p (c t) -> p c t", c=8)
        bc = p0["bc_ms"]
        for c in range(8):
            V(lambda v, c=c, x323=x323: v.tensor_tensor(
                x323[:, c, :], x323[:, c, :], p_b1[:, :], ALU.add),
              waits=[(tM, bc), (qX0[p], p0["xq"][p])] if c == 0 else ())
            V(lambda v, c=c, x323=x323, xnb3=xnb3: v.tensor_tensor(
                xnb3[:, c, :], x323[:, c, :], p_b2[:, :], ALU.mult))
            V(lambda v, c=c, cslice=cslice, x323=x323: v.scalar_tensor_tensor(
                xa3[:, c, cslice], x323[:, c, :], s_g1[:, c:c + 1], p_b3[:, :],
                ALU.mult, ALU.mult))
        apl_ms = vC.n
        p0["apl_v"][p] = apl_ms
        DA(qXN[p], lambda s, cslice=cslice, p=p: s.dma_start(
            xn_w[:, :, cslice], s_xnb[p][:, :].rearrange("p (c t) -> p c t", c=8)),
           waits=[(vC, apl_ms)])
        p0["xnst"][p] = qXN[p].n

    p0_load(0)
    p0_stats(0)
    for blk in range(NB):
        if blk + 1 < NB:
            p0_load(blk + 1)
        p0_rows(blk)
        if blk + 1 < NB:
            p0_stats(blk + 1)
        p0_apply(blk)

    barrier()
    st0b.close()
    if upto == 0:
        _finish(nc, P, _dma_sems)
        st0.close()
        ctx.close()
        return nc

    st3pre = ExitStack()
    s_g4 = sb("s_g4", [128, 4 * D], BF16, st3pre)
    g43 = s_g4[:, :].rearrange("p (n d) -> p n d", n=4)
    s_wo_a = sb("s_wo_a", [128, 4 * 1024], BF16, st3pre)
    wo3a = s_wo_a[:, :].rearrange("p (k o) -> p k o", k=4)
    st_kv = ExitStack()
    s_K = sb("s_K", [128, 8 * TOT], BF16, st_kv)
    K3 = s_K[:, :].rearrange("p (c t) -> p c t", c=8)
    s_V = sb("s_V", [128, NT * H * 65], BF16, st_kv)
    V4 = s_V[:, :].rearrange("p (n h e) -> p n h e", n=NT, h=H)
    st1b = ExitStack()
    s_w8 = sb("s_w8", [128, 8 * 512], BF16, st1b, side="right")
    w83 = s_w8[:, :].rearrange("p (k o) -> p k o", k=8)
    s_evq = sb("s_evq", [128, 1024], BF16, st1b, side="right")
    s_gt32 = sb("s_gt32", [128, 512], F32, st1b, side="right")
    s_vc = sb("s_vc", [128, 1], F32, st1b, side="right")
    pg = [ps("p_gm0", [128, 1024], F32, st1b), ps("p_gm1", [128, 1024], F32, st1b)]
    wSL = DS("wSL")
    qES = DS("qES")
    pg_free = [0, 0]
    pi = 0

    wqkv_v = wqkv.ap().rearrange("(k p) o -> p k o", p=128)
    wgate_v = wgate.ap().rearrange("(k p) o -> p k o", p=128)

    for (base_col, is_q) in ((D, False), (0, True)):
        for wh in range(2):
            DW(wSL, lambda g, base_col=base_col, wh=wh: g.dma_start(
                w83[:, :, :], wqkv_v[:, :, base_col + wh * 512: base_col + (wh + 1) * 512]),
               waits=[(tM, tM.n)])
            wslabq = wSL.n
            grps = [(3, 2), (5, 2)] if is_q else [(0, 2), (2, 2), (4, 2), (6, 1)]
            for oc4 in range(4):
                oc = wh * 4 + oc4
                for (b0, gn) in grps:
                    p = pg[pi]
                    fw = [(vC, pg_free[pi])] + ([(wSL, wslabq)] if (oc4 == 0 and b0 == grps[0][0]) else [])
                    for kc in range(8):
                        for b in range(gn):
                            T(lambda t, oc4=oc4, kc=kc, b0=b0, b=b, p=p: t.matmul(
                                p[:, b * 512:(b + 1) * 512],
                                w83[:, kc, oc4 * 128:(oc4 + 1) * 128],
                                xa3[:, kc, (b0 + b) * 512:(b0 + b + 1) * 512],
                                start=(kc == 0), stop=(kc == 7)),
                              incs=[(tM, 1)] if (kc == 7 and b == gn - 1) else (),
                              waits=fw if (kc == 0 and b == 0) else ())
                    mm_ms = tM.n
                    if is_q:
                        V(lambda v, p=p: v.tensor_scalar_mul(s_evq[:, :], p[:, :], 0.125),
                          waits=[(tM, mm_ms), (qES, qES.n)])
                        ev_ms = vC.n
                        DA(qES, lambda s, oc=oc, b0=b0: s.dma_start(
                            d_q.ap()[oc][:, (b0 - 3) * 512:(b0 - 1) * 512], s_evq[:, :]),
                           waits=[(vC, ev_ms)])
                    else:
                        V(lambda v, oc=oc, b0=b0, gn=gn, p=p: v.tensor_copy(
                            K3[:, oc, b0 * 512:(b0 + gn) * 512], p[:, 0:gn * 512]),
                          waits=[(tM, mm_ms)])
                        ev_ms = vC.n
                    pg_free[pi] = ev_ms
                    pi ^= 1

    for wh in range(2):
        DW(wSL, lambda g, wh=wh: g.dma_start(
            w83[:, :, :], wgate_v[:, :, wh * 512:(wh + 1) * 512]),
           waits=[(tM, tM.n)])
        wgq = wSL.n
        for tt in range(QT_N):
            p = pg[pi]
            fw = [(vC, pg_free[pi])] + ([(wSL, wgq)] if tt == 0 else [])
            for kc in range(8):
                T(lambda t, tt=tt, kc=kc, p=p: t.matmul(
                    p[:, 0:512],
                    xa3[:, kc, HALO + tt * 128: HALO + (tt + 1) * 128],
                    w83[:, kc, :],
                    start=(kc == 0), stop=(kc == 7)),
                  incs=[(tM, 1)] if kc == 7 else (),
                  waits=fw if kc == 0 else ())
            mm_ms = tM.n
            V(lambda v, p=p, wh=wh: v.tensor_tensor(
                s_gt32[:, :], p[:, 0:512], s_bgbc[:, wh * 512:(wh + 1) * 512], ALU.add),
              waits=[(tM, mm_ms), (sC, sC.n)])
            pg_free[pi] = vC.n
            pi ^= 1
            vv = vC.n
            SE(lambda s: s.activation(s_evq[:, 0:512], s_gt32[:, :], AF.Sigmoid),
               waits=[(vC, vv), (qES, qES.n)])
            sv = sC.n
            DA(qES, lambda s, tt=tt, wh=wh: s.dma_start(
                d_gate.ap()[tt][:, wh * 512:(wh + 1) * 512], s_evq[:, 0:512]),
               waits=[(sC, sv)])

    for wh in range(2):
        DW(wSL, lambda g, wh=wh: g.dma_start(
            w83[:, :, :], wqkv_v[:, :, 2 * D + wh * 512: 2 * D + (wh + 1) * 512]),
           waits=[(tM, tM.n)])
        wvq = wSL.n
        for tt in range(NT):
            p = pg[pi]
            fw = [(vC, pg_free[pi])] + ([(wSL, wvq)] if tt == 0 else [])
            for kc in range(8):
                T(lambda t, tt=tt, kc=kc, p=p: t.matmul(
                    p[:, 0:512],
                    xa3[:, kc, tt * 128:(tt + 1) * 128],
                    w83[:, kc, :],
                    start=(kc == 0), stop=(kc == 7)),
                  incs=[(tM, 1)] if kc == 7 else (),
                  waits=fw if kc == 0 else ())
            mm_ms = tM.n
            V(lambda v, tt=tt, p=p, wh=wh: v.tensor_copy(
                V4[:, tt, wh * 8:(wh + 1) * 8, 0:64],
                p[:, 0:512].rearrange("p (h d) -> p h d", h=8)),
              waits=[(tM, mm_ms)])
            V(lambda v, tt=tt: v.tensor_single_scalar(
                s_vc[:, :], s_tvec[:, :], float(HALO - tt * 128), ALU.is_ge))
            V(lambda v, tt=tt, wh=wh: v.tensor_copy(
                V4[:, tt, wh * 8:(wh + 1) * 8, 64:65],
                s_vc[:, :].unsqueeze(1).broadcast_to((128, 8, 1))))
            pg_free[pi] = vC.n
            pi ^= 1

    barrier()
    st1b.close()
    st0.close()
    if upto == 1:
        _finish(nc, P, _dma_sems)
        st_kv.close()
        ctx.close()
        return nc

    qMT = DS("qMT")
    qQT = [DS("qQT0"), DS("qQT1")]
    qGT = [DS("qGT0"), DS("qGT1")]
    qGD = [DS("qGD0"), DS("qGD1")]
    qNB = [DS("qNB0"), DS("qNB1")]
    st2 = ExitStack()
    s_MT = sb("s_MT", [128, H * NJ * 128], BF16, st2)
    MT3 = s_MT[:, :].rearrange("p (h x) -> p h x", h=H)
    MT4 = s_MT[:, :].rearrange("p (h j q) -> p h j q", h=H, j=NJ)
    _offs_mt = [127 - 128 * J[jj] for jj in range(NJ)]
    for h in range(H):
        for (i0, stp, cnt_) in _offset_runs(_offs_mt):
            def mdma(s, h=h, i0=i0, stp=stp, cnt_=cnt_):
                src = bass.AP(d_grep, h * 128 * GL + _offs_mt[i0],
                              [[GL - 1, 128], [stp, cnt_], [1, 128]])
                return s.dma_start(MT4[:, h, i0:i0 + cnt_, :], src)
            DA(qMT, mdma, waits=[(qG, grep_q)] if h == 0 else ())
    mt_q = qMT.n

    s_qt = [sb("s_qt0", [128, 8 * 128], BF16, st2), sb("s_qt1", [128, 8 * 128], BF16, st2)]
    s_gtt = [sb("s_gtt0", [128, D], BF16, st2), sb("s_gtt1", [128, D], BF16, st2)]
    s_gd = [sb("s_gd0", [128, D], BF16, st2), sb("s_gd1", [128, D], BF16, st2)]
    s_b = [sb("s_b0", [128, 16], F32, st2), sb("s_b1", [128, 16], F32, st2)]
    s_bq = sb("s_bq", [128, 1], F32, st2)
    s_nbm = sb("s_nbm", [1, 1], F32, st2)
    s_negB = [sb("s_negB0", [128, 1], F32, st2), sb("s_negB1", [128, 1], F32, st2)]
    s_E = [sb("s_E0", [128, NJ * 128], BF16, st2), sb("s_E1", [128, NJ * 128], BF16, st2)]
    s_at = [sb("s_at0", [128, NJ * 128], BF16, st2), sb("s_at1", [128, NJ * 128], BF16, st2)]
    s_rd = sb("s_rd", [128, 1], F32, st2)
    p_A = [ps("p_A0", [128, 256], F32, st2), ps("p_A1", [128, 256], F32, st2)]
    p_B = [ps("p_B0", [128, NJ * 128], F32, st2), ps("p_B1", [128, NJ * 128], F32, st2)]
    p_av = [ps("p_av0", [128, 65], F32, st2), ps("p_av1", [128, 65], F32, st2)]
    s_nbp = sb("s_nbp", [128, 128], BF16, st2)
    s_nbTp = sb("s_nbTp", [128, 128], BF16, st2)

    bstride = 1 - min(J)
    st_p2 = {
        "pB_free": [0, 0],
        "mask_done": [0, 0],
        "av_done_t": [0, 0],
        "at_ms": {},
        "pb_ms": {},
        "exp_ms": {},
        "qld": [0, 0],
        "gld": [0, 0],
        "negb_q": [0, 0],
        "gd_store": [0, 0],
        "pA_guard": 0,
        "av_free": [0, 0],
        "exp_done": [0, 0],
    }

    def p2_tile_loads(qt):
        bi = qt & 1
        DA(qQT[bi], lambda s, qt=qt, bi=bi: s.dma_start(
            s_qt[bi][:, :].rearrange("p (c t) -> p c t", c=8),
            d_q.ap().rearrange("c p t -> p c t")[:, :, qt * 128:(qt + 1) * 128]),
           waits=[(tM, tM.n)])
        st_p2["qld"][bi] = qQT[bi].n
        DA(qGT[bi], lambda s, qt=qt, bi=bi: s.dma_start(s_gtt[bi][:, :], d_gate.ap()[qt]),
           waits=[(vC, vC.n)])
        st_p2["gld"][bi] = qGT[bi].n

    def p2_passA_grp(qt, grp):
        bi = qt & 1
        q3 = s_qt[bi][:, :].rearrange("p (c t) -> p c t", c=8)
        t0col = HALO + qt * 128
        start_col = t0col + 128 - bstride * 128
        for hh in range(4):
            h = grp * 4 + hh
            g, r0 = h // 2, (h % 2) * 64
            par2 = h % 2
            fw = []
            if hh == 0:
                fw = [(vC, st_p2["pA_guard"]), (qQT[bi], st_p2["qld"][bi])]
            T(lambda t, hh=hh, g=g, r0=r0, par2=par2, start_col=start_col, q3=q3: t.matmul(
                p_A[par2][:, (hh // 2) * 128:(hh // 2 + 1) * 128],
                q3[r0:r0 + 64, g, :],
                K3[r0:r0 + 64, g, start_col:start_col + bstride * 128:bstride],
                start=(hh < 2), stop=(hh >= 2),
                tile_position=(r0, 0)),
              incs=[(tM, 1)] if hh == 3 else (),
              waits=fw if hh == 0 else ())
        amss = tM.n
        for par2 in range(2):
            V(lambda v, qt=qt, bi=bi, grp=grp, par2=par2: v.tensor_reduce(
                s_b[bi][:, grp * 4 + par2: (grp + 1) * 4: 2],
                p_A[par2][:, :].rearrange("p (h k) -> p h k", h=2),
                axis=AX.X, op=ALU.max),
              waits=[(tM, amss)] if par2 == 0 else ())
        st_p2["pA_guard"] = vC.n

    def p2_passA_tail(qt):
        bi = qt & 1
        V(lambda v, bi=bi: v.tensor_reduce(s_bq[:, :], s_b[bi][:, :], axis=AX.X, op=ALU.max),
          waits=[(qNB[bi ^ 1], qNB[bi ^ 1].n), (qNB[bi], qNB[bi].n)])
        V(lambda v: v.tensor_scalar_mul(s_nbp[:, 0:1], s_bq[:, :], -1.0))
        nb_ms = vC.n
        DA(qNB[bi], lambda s: s.dma_start(s_nbTp[:, :], s_nbp[:, :], transpose=True),
           waits=[(vC, nb_ms)])
        nbq = qNB[bi].n
        V(lambda v, nbq=nbq: v.tensor_reduce(
            s_nbm[:, :], s_nbTp[0:1, :], axis=AX.X, op=ALU.min),
          waits=[(qNB[bi], nbq)])
        nbm_v = vC.n
        DA(qNB[bi], lambda s, bi=bi: s.dma_start(d_negb.ap()[bi:bi + 1].unsqueeze(0), s_nbm[:, :]),
           waits=[(vC, nbm_v)])
        DA(qNB[bi], lambda s, bi=bi: s.dma_start(
            s_negB[bi][:, :], bass.AP(d_negb, bi, [[0, 128], [1, 1]])),
           waits=[(qNB[bi], qNB[bi].n), (sC, sC.n)])
        st_p2["negb_q"][bi] = qNB[bi].n

    HW2 = (NJ * 128) // 2

    def p2_passB_mm(qt, h):
        bi = qt & 1
        par = h & 1
        g, r0 = h // 2, (h % 2) * 64
        pB = p_B[par]
        q3 = s_qt[bi][:, :].rearrange("p (c t) -> p c t", c=8)
        fw = [(sC, st_p2["exp_done"][par])] + \
             ([(qQT[bi], st_p2["qld"][bi])] if h == 0 else [])
        for jj in range(NJ):
            ktile = HT + qt + J[jj]
            T(lambda t, jj=jj, ktile=ktile, g=g, r0=r0, pB=pB, q3=q3: t.matmul(
                pB[:, jj * 128:(jj + 1) * 128],
                K3[r0:r0 + 64, g, ktile * 128:(ktile + 1) * 128],
                q3[r0:r0 + 64, g, :],
                start=(jj % (NJ // 2) == 0), stop=(jj % 4 == 3),
                tile_position=(r0, 0)),
              incs=[(tM, 1)] if jj == NJ - 1 else (),
              waits=fw if jj == 0 else ())
        st_p2["pb_ms"][h] = tM.n
        eh = []
        for hx in range(2):
            SE(lambda s, pB=pB, par=par, hx=hx, bi=bi: s.activation(
                s_E[par][:, hx * HW2:(hx + 1) * HW2], pB[:, hx * HW2:(hx + 1) * HW2], AF.Exp,
                bias=s_negB[bi][:, :]),
               waits=[(tM, st_p2["pb_ms"][h]), (qNB[bi], st_p2["negb_q"][bi])] +
                     ([(vC, st_p2["mask_done"][par])] if hx == 0 else []))
            eh.append(sC.n)
        st_p2["exp_done"][par] = eh[1]
        st_p2["exp_ms"][h] = tuple(eh)

    def p2_mask(qt, h):
        par = h & 1
        (e1, e2) = st_p2["exp_ms"][h]
        at_half = []
        for hx in range(2):
            V(lambda v, h=h, par=par, hx=hx: v.tensor_tensor(
                s_at[par][:, hx * HW2:(hx + 1) * HW2],
                s_E[par][:, hx * HW2:(hx + 1) * HW2],
                MT3[:, h, hx * HW2:(hx + 1) * HW2], ALU.mult),
              waits=[(sC, (e1 if hx == 0 else e2)), (qMT, mt_q)] +
                    ([(tM, st_p2["av_done_t"][par])] if hx == 0 else []))
            at_half.append(vC.n)
        st_p2["at_ms"][h] = tuple(at_half)
        st_p2["mask_done"][par] = vC.n

    def p2_av(qt, h):
        bi = qt & 1
        par = h & 1
        (at1, at2) = st_p2["at_ms"][h]
        pav = p_av[par]
        for jj in range(NJ):
            ktile = HT + qt + J[jj]
            fw = ()
            if jj == 0:
                fw = [(vC, at1), (vC, st_p2["av_free"][par])]
            elif jj == NJ // 2:
                fw = [(vC, at2)]
            T(lambda t, jj=jj, ktile=ktile, h=h, pav=pav: t.matmul(
                pav[:, 0:65],
                s_at[par][:, jj * 128:(jj + 1) * 128],
                V4[:, ktile, h, :],
                start=(jj == 0), stop=(jj == NJ - 1)),
              incs=[(tM, 1)] if jj == NJ - 1 else (),
              waits=fw)
        av_ms = tM.n
        st_p2["av_done_t"][par] = av_ms
        V(lambda v, pav=pav: v.reciprocal(s_rd[:, :], pav[:, 64:65]),
          waits=[(tM, av_ms)] +
                ([(qGT[bi], st_p2["gld"][bi]), (qGD[bi], st_p2["gd_store"][bi])] if h == 0 else []))
        V(lambda v, h=h, bi=bi, pav=pav: v.scalar_tensor_tensor(
            s_gd[bi][:, h * 64:(h + 1) * 64], pav[:, 0:64], s_rd[:, :],
            s_gtt[bi][:, h * 64:(h + 1) * 64], ALU.mult, ALU.mult))
        st_p2["av_free"][par] = vC.n

    V(lambda v: v.memset(s_nbp[:, :], 0.0))
    qG4 = DS("qG4")
    qWO = DS("qWO")
    DW(qWO, lambda g: g.dma_start(
        wo3a[:, :, :], wout.ap().rearrange("(k p) o -> p k o", p=128)[:, 0:4, :]))
    p3_pre = {"g4q0": 0, "woq": qWO.n}
    p2_tile_loads(0)
    for grp in range(4):
        p2_passA_grp(0, grp)
    p2_passA_tail(0)
    for qt in range(QT_N):
        bi = qt & 1
        for h in range(H):
            p2_passB_mm(qt, h)
            if h >= 1:
                p2_av(qt, h - 1)
            p2_mask(qt, h)
            if qt + 1 < QT_N:
                if h == 0:
                    p2_tile_loads(qt + 1)
                elif 1 <= h <= 4:
                    p2_passA_grp(qt + 1, h - 1)
                elif h == 5:
                    p2_passA_tail(qt + 1)
        p2_av(qt, H - 1)
        gd_ms = vC.n
        DA(qGD[bi], lambda s, qt=qt, bi=bi: s.dma_start(d_gated.ap()[qt], s_gd[bi][:, :]),
           waits=[(vC, gd_ms)])
        st_p2["gd_store"][bi] = qGD[bi].n
        if qt == 3:
            DA(qG4, lambda s: s.dma_start(
                g43[:, :, :], d_gated.ap().rearrange("n p d -> p n d")[:, 0:4, :]),
               waits=[(qGD[0], st_p2["gd_store"][0]), (qGD[1], st_p2["gd_store"][1])])
            p3_pre["g4q0"] = qG4.n

    if tap is not None and tap[0] == "gated":
        barrier()
        DA(qC, lambda s: s.dma_start(tap_t.ap(), d_gated.ap()))
    barrier()
    st2.close()
    st_kv.close()
    if upto == 2:
        _finish(nc, P, _dma_sems)
        ctx.close()
        return nc

    st3 = ExitStack()
    s_xn2 = sb("s_xn2", [128, 8 * OWN], BF16, st3)
    xn23 = s_xn2[:, :].rearrange("p (c t) -> p c t", c=8)
    st3b = ExitStack()
    s_wo_b = sb("s_wo_b", [128, 4 * 1024], BF16, st3b)
    wo3b = s_wo_b[:, :].rearrange("p (k o) -> p k o", k=4)
    s_wo_b32 = sb("s_wo_b32", [128, 4 * 1024], F32, st3b)
    s_gT = sb("s_gT", [128, 8 * 512], BF16, st3b)
    gT3 = s_gT[:, :].rearrange("p (c t) -> p c t", c=8)
    s_xnb3_ = sb("s_xnb3", [128, 8 * 512], BF16, st3b)
    xnb33 = s_xnb3_[:, :].rearrange("p (c t) -> p c t", c=8)
    s_r1 = sb("s_r1", [128, 8 * 512], F32, st3b)
    r13 = s_r1[:, :].rearrange("p (c t) -> p c t", c=8)
    s_r1b = sb("s_r1b", [128, 8 * 512], BF16, st3b)
    s_row3 = sb("s_row3", [1, 512], F32, st3b)
    r1b3 = s_r1b[:, :].rearrange("p (c t) -> p c t", c=8)
    p_t3 = [ps("p_t30", [128, 128], BF16, st3b), ps("p_t31", [128, 128], BF16, st3b)]
    p_o3 = [ps("p_o30", [128, 512], F32, st3b), ps("p_o31", [128, 512], F32, st3b)]
    p_sr = ps("p_sr", [1, 512], F32, st3b)
    p_bc3 = ps("p_bc3", [128, 512], F32, st3b)

    qXN3 = DS("qXN3")
    qR1S = DS("qR1S")
    qC2 = DS("qC2")
    for c in range(8):
        DW(qC2, lambda s, c=c: s.dma_start(s_g2[:, c:c + 1], g2v.ap()[c * 128:(c + 1) * 128].unsqueeze(1)))
        DW(qC2, lambda s, c=c: s.dma_start(s_s3[:, c:c + 1], s3v.ap()[c * 128:(c + 1) * 128].unsqueeze(1)))
    for c in range(32):
        DW(qC2, lambda s, c=c: s.dma_start(s_s1[:, c:c + 1], s1v.ap()[c * 128:(c + 1) * 128].unsqueeze(1)))
        DW(qC2, lambda s, c=c: s.dma_start(s_s2[:, c:c + 1], s2v.ap()[c * 128:(c + 1) * 128].unsqueeze(1)))
        DW(qC2, lambda s, c=c: s.dma_start(s_sh1[:, c:c + 1], sh1v.ap()[c * 128:(c + 1) * 128].unsqueeze(1)))
        DW(qC2, lambda s, c=c: s.dma_start(s_sh2[:, c:c + 1], sh2v.ap()[c * 128:(c + 1) * 128].unsqueeze(1)))
    qc2_q = qC2.n
    woq = p3_pre["woq"]
    qWOB = DS("qWOB")
    DSC(qWOB, lambda s: s.dma_start(
        s_wo_b32[:, :].rearrange("p (k o) -> p k o", k=4),
        wout.ap().rearrange("(k p) o -> p k o", p=128)[:, 4:8, :]),
        waits=[(tM, tM.n)])
    wob_q = qWOB.n
    V(lambda v: v.tensor_copy(s_wo_b[:, :], s_wo_b32[:, :]), waits=[(qWOB, wob_q)])
    wob_v = vC.n
    pt_free = [0, 0]
    po_free = [0, 0]
    prev_r1_store = 0
    sr_evac_v = 0
    ti = 0
    p2_end_v = vC.n
    p2_end_s = sC.n
    for blk in range(4):
        if blk == 0:
            g4q = p3_pre["g4q0"]
        else:
            DA(qG4, lambda s, blk=blk: s.dma_start(
                g43[:, :, :],
                d_gated.ap().rearrange("n p d -> p n d")[:, blk * 4:(blk + 1) * 4, :]),
               waits=[(tM, tM.n)])
            g4q = qG4.n
        DA(qXN3, lambda s, blk=blk: s.dma_start(
            xnb33[:, :, :],
            xn_w[:, :, HALO + blk * 512: HALO + (blk + 1) * 512]),
           waits=[(vC, vC.n), (qXN[0], p0["xnst"][0]), (qXN[1], p0["xnst"][1])])
        xnq = qXN3.n
        for ttl in range(4):
            for c in range(8):
                pt = p_t3[ti]
                fw = [(vC, pt_free[ti])] + ([(qG4, g4q)] if (ttl == 0 and c == 0) else [])
                if blk == 0 and ttl == 0 and c == 0:
                    fw += [(vC, p2_end_v), (sC, p2_end_s)]
                T(lambda t, ttl=ttl, c=c, pt=pt: t.transpose(
                    pt[:, :], g43[:, ttl, c * 128:(c + 1) * 128], s_id[:, :]),
                  incs=[(tM, 1)], waits=fw)
                trm = tM.n
                V(lambda v, ttl=ttl, c=c, pt=pt: v.tensor_copy(
                    gT3[:, c, ttl * 128:(ttl + 1) * 128], pt[:, :]),
                  waits=[(tM, trm)])
                pt_free[ti] = vC.n
                ti ^= 1
        gT_ms = vC.n
        for oc in range(8):
            po = p_o3[oc & 1]
            fw = [(vC, max(po_free[oc & 1], gT_ms))] + ([(qWO, woq)] if (blk == 0 and oc == 0) else [])
            fw4 = [(vC, wob_v)] if (blk == 0 and oc == 0) else ()
            for kc in range(8):
                wv = wo3a[:, kc, oc * 128:(oc + 1) * 128] if kc < 4 else                      wo3b[:, kc - 4, oc * 128:(oc + 1) * 128]
                T(lambda t, kc=kc, po=po, wv=wv: t.matmul(
                    po[:, :], wv, gT3[:, kc, :],
                    start=(kc == 0), stop=(kc == 7)),
                  incs=[(tM, 1)] if kc == 7 else (),
                  waits=fw if kc == 0 else (fw4 if kc == 4 else ()))
            omm = tM.n
            V(lambda v, oc=oc, po=po: v.tensor_tensor(
                r13[:, oc, :], po[:, :], xnb33[:, oc, :], ALU.add),
              waits=[(tM, omm), (qXN3, xnq)] + ([(qR1S, prev_r1_store)] if oc == 0 else []))
            po_free[oc & 1] = vC.n
        r1_ms = vC.n
        DA(qR1S, lambda s, blk=blk: s.dma_start(
            d_r1.ap().rearrange("c p t -> p c t")[:, :, blk * 512:(blk + 1) * 512],
            r13[:, :, :]),
           waits=[(vC, r1_ms)])
        prev_r1_store = qR1S.n
        V(lambda v: v.tensor_copy(s_r1b[:, :], s_r1[:, :]))
        V(lambda v: v.tensor_mul(s_r1b[:, :], s_r1b[:, :], s_r1b[:, :]))
        sq_ms = vC.n
        for c in range(8):
            T(lambda t, c=c: t.matmul(p_sr[:, :], s_onescb[:, :], r1b3[:, c, :],
                                      start=(c == 0), stop=(c == 7)),
              incs=[(tM, 1)] if c == 7 else (),
              waits=[(vC, max(sq_ms, sr_evac_v))] if c == 0 else ())
        srm = tM.n
        V(lambda v: v.tensor_copy(s_row3[:, :], p_sr[:, :]), waits=[(tM, srm)])
        sr_evac_v = vC.n
        V(lambda v: v.tensor_scalar_mul(s_row3[:, :], s_row3[:, :], 1.0 / D))
        vr = vC.n
        SE(lambda s: s.activation(s_row3[:, :], s_row3[:, :], AF.Ln, bias=s_cv[0:1, 1:2]), waits=[(vC, vr)])
        SE(lambda s: s.activation(s_row3[:, :], s_row3[:, :], AF.Exp, scale=s_cv[0:1, 2:3]))
        rr_ms = sC.n
        T(lambda t: t.matmul(p_bc3[:, :], s_onesf[:, :], s_row3[:, :], start=True, stop=True),
          incs=[(tM, 1)], waits=[(sC, rr_ms), (vC, vC.n)])
        bc_ms = tM.n
        for c in range(8):
            V(lambda v, c=c, blk=blk: v.scalar_tensor_tensor(
                xn23[:, c, blk * 512:(blk + 1) * 512], r13[:, c, :], s_g2[:, c:c + 1],
                p_bc3[:, :], ALU.mult, ALU.mult),
              waits=([(tM, bc_ms)] + ([(qC2, qc2_q)] if blk == 0 else [])) if c == 0 else ())

    barrier()
    st3b.close()
    if upto == 3:
        _finish(nc, P, _dma_sems)
        st3.close()
        st3pre.close()
        ctx.close()
        return nc

    st4 = ExitStack()
    s_hatt = sb("s_hatt", [128, 22 * OWN], BF16, st4)
    ha3 = s_hatt[:, :].rearrange("p (c t) -> p c t", c=22)
    st4w = ExitStack()
    s_wga = sb("s_wga", [128, 8 * FFA], BF16, st4w)
    wga3 = s_wga[:, :].rearrange("p (k o) -> p k o", k=8)
    p_f = [ps("p_f0", [128, 2048], F32, st4), ps("p_f1", [128, 2048], F32, st4)]
    pf_free_v = [0, 0]
    pf_free_s = [0, 0]

    wfg_v = wfg.ap().rearrange("(k p) o -> p k o", p=128)
    wfu_v = wfu.ap().rearrange("(k p) o -> p k o", p=128)
    qR14 = DS("qR14")
    qR2S = DS("qR2S")
    QB4 = [(0, 6), (6, 12), (12, 17), (17, 22)]
    qW4 = DS("qW4")
    wgaq_q = []
    for (a, b) in QB4:
        DW(qW4, lambda g, a=a, b=b: g.dma_start(
            wga3[:, :, a * 128:b * 128], wfg_v[:, :, a * 128:b * 128]),
           waits=([(tM, tM.n)] if a == 0 else []) + [(qW4, qW4.n)])
        wgaq_q.append(qW4.n)
    def _q4(oc):
        for qi, (a, b) in enumerate(QB4):
            if oc < b:
                return qi
    wfg_oc_t = {}
    for oc in range(22):
        pf = p_f[oc & 1]
        fw = [(sC, pf_free_s[oc & 1]), (qW4, wgaq_q[_q4(oc)])]
        for kc in range(8):
            for b in range(4):
                T(lambda t, oc=oc, kc=kc, b=b, pf=pf: t.matmul(
                    pf[:, b * 512:(b + 1) * 512],
                    wga3[:, kc, oc * 128:(oc + 1) * 128],
                    xn23[:, kc, b * 512:(b + 1) * 512],
                    start=(kc == 0), stop=(kc == 7)),
                  incs=[(tM, 1)] if (kc == 7 and b == 3) else (),
                  waits=fw if (kc == 0 and b == 0) else ())
        fm = tM.n
        wfg_oc_t[oc] = fm
        SE(lambda s, oc=oc, pf=pf: s.activation(ha3[:, oc, :], pf[:, :], AF.Silu),
           waits=[(tM, fm)])
        pf_free_s[oc & 1] = sC.n
    g_done_t = tM.n
    wuq_q = []
    for (a, b) in QB4:
        DW(qW4, lambda g, a=a, b=b: g.dma_start(
            wga3[:, :, a * 128:b * 128], wfu_v[:, :, a * 128:b * 128]),
           waits=[(tM, wfg_oc_t[b - 1]), (qW4, qW4.n)])
        wuq_q.append(qW4.n)
    for oc in range(22):
        pf = p_f[oc & 1]
        fw = [(vC, pf_free_v[oc & 1]), (sC, sC.n), (qW4, wuq_q[_q4(oc)])]
        for kc in range(8):
            for b in range(4):
                T(lambda t, oc=oc, kc=kc, b=b, pf=pf: t.matmul(
                    pf[:, b * 512:(b + 1) * 512],
                    wga3[:, kc, oc * 128:(oc + 1) * 128],
                    xn23[:, kc, b * 512:(b + 1) * 512],
                    start=(kc == 0), stop=(kc == 7)),
                  incs=[(tM, 1)] if (kc == 7 and b == 3) else (),
                  waits=fw if (kc == 0 and b == 0) else ())
        fm = tM.n
        V(lambda v, oc=oc, pf=pf: v.tensor_tensor(ha3[:, oc, :], ha3[:, oc, :], pf[:, :], ALU.mult),
          waits=[(tM, fm)])
        pf_free_v[oc & 1] = vC.n
    u_done_t = tM.n
    u_done_v = vC.n
    barrier()
    st4w.close()

    s_wd = sb("s_wd", [128, 22 * D], BF16, st4)
    wd3 = s_wd[:, :].rearrange("p (k o) -> p k o", k=22)
    s_r1s = sb("s_r1s", [128, OWN], F32, st4)
    s_r2s = sb("s_r2s", [128, OWN], F32, st4)
    wfd_v = wfd.ap().rearrange("(k p) o -> p k o", p=128)
    wdq_q = []
    for q in range(4):
        DW(qW4, lambda g, q=q: g.dma_start(
            wd3[:, :, q * 256:(q + 1) * 256], wfd_v[:, :, q * 256:(q + 1) * 256]),
           waits=([(tM, u_done_t)] if q == 0 else []) + [(qW4, qW4.n)])
        wdq_q.append(qW4.n)
    for oc in range(8):
        pf = p_f[oc & 1]
        DA(qR14, lambda s, oc=oc: s.dma_start(s_r1s[:, :], d_r1.ap()[oc]),
           waits=[(vC, vC.n)])
        r1q = qR14.n
        fw = [(vC, max(pf_free_v[oc & 1], u_done_v)), (qW4, wdq_q[oc // 2])]
        for kc in range(22):
            for b in range(4):
                T(lambda t, oc=oc, kc=kc, b=b, pf=pf: t.matmul(
                    pf[:, b * 512:(b + 1) * 512],
                    wd3[:, kc, oc * 128:(oc + 1) * 128],
                    ha3[:, kc, b * 512:(b + 1) * 512],
                    start=(kc == 0), stop=(kc == 21)),
                  incs=[(tM, 1)] if (kc == 21 and b == 3) else (),
                  waits=fw if (kc == 0 and b == 0) else ())
        fm = tM.n
        V(lambda v, pf=pf: v.tensor_tensor(s_r2s[:, :], pf[:, :], s_r1s[:, :], ALU.add),
          waits=[(tM, fm), (qR14, r1q), (qR2S, qR2S.n)])
        pf_free_v[oc & 1] = vC.n
        vv = vC.n
        DA(qR2S, lambda s, oc=oc: s.dma_start(d_r2.ap()[oc], s_r2s[:, :]), waits=[(vC, vv)])

    barrier()
    st4.close()
    st3.close()
    st3pre.close()
    if upto == 4:
        _finish(nc, P, _dma_sems)
        ctx.close()
        return nc

    st5 = ExitStack()
    s_ha5 = sb("s_ha5", [128, 32 * 1024], BF16, st5)
    h53 = s_ha5[:, :].rearrange("p (c t) -> p c t", c=32)
    s_hb5 = sb("s_hb5", [128, 32 * 1024], BF16, st5)
    hb53 = s_hb5[:, :].rearrange("p (c t) -> p c t", c=32)
    xh3 = s_hb5[:, 0:8 * 1024].rearrange("p (c t) -> p c t", c=8)
    s_hs = [sb("s_hs0", [128, 32 * 256], BF16, st5), sb("s_hs1", [128, 32 * 256], BF16, st5)]
    s_rr = sb("s_rr", [128, 1024], F32, st5)
    s_xx = sb("s_xx", [128, 1024], F32, st5)
    s_oo = sb("s_oo", [128, 1024], F32, st5)
    p_5 = [ps("p_50", [128, 1024], F32, st5), ps("p_51", [128, 1024], F32, st5),
           ps("p_52", [128, 1024], F32, st5), ps("p_53", [128, 1024], F32, st5)]

    w1_v = w1T.ap().rearrange("(k p) o -> p k o", p=128)
    w2_v = w2T.ap().rearrange("(k p) o -> p k o", p=128)
    w3_v = w3T.ap().rearrange("(k p) o -> p k o", p=128)
    xTv2 = xT.ap().rearrange("(c p) t -> p c t", p=128)
    out_v = out_t.ap().rearrange("(c p) t -> p c t", p=128)

    wS5 = [DS("wS50"), DS("wS51")]
    qXH = DS("qXH")
    qRX = DS("qRX")
    qOUT = DS("qOUT")
    p5_free_v = [0] * 4
    p5_free_s = [0] * 4
    hs_free_t = [0, 0]
    state = {"fm": 0}

    def gemm_slab(wv, n_kc, src3, n_oc, evac, gate=()):
        gate = list(gate)
        for og in range(n_oc // 2):
            sl3 = s_hs[og & 1][:, 0:n_kc * 256].rearrange("p (k o) -> p k o", k=n_kc)
            DW(wS5[og & 1], lambda g, og=og, wv=wv, sl3=sl3: g.dma_start(
                sl3[:, :, :], wv[:, :, og * 256:(og + 1) * 256]),
               waits=[(tM, hs_free_t[og & 1])])
            slq = wS5[og & 1].n
            for o2 in range(2):
                oc = og * 2 + o2
                pp = p_5[oc & 3]
                fw = [(wS5[og & 1], slq), (vC, p5_free_v[oc & 3]), (sC, p5_free_s[oc & 3])]
                if og == 0 and o2 == 0:
                    fw = fw + gate
                for kc in range(n_kc):
                    for b in range(2):
                        T(lambda t, o2=o2, kc=kc, b=b, pp=pp, src3=src3, sl3=sl3, n_kc=n_kc: t.matmul(
                            pp[:, b * 512:(b + 1) * 512],
                            sl3[:, kc, o2 * 128:(o2 + 1) * 128],
                            src3[:, kc, b * 512:(b + 1) * 512],
                            start=(kc == 0), stop=(kc == n_kc - 1)),
                          incs=[(tM, 1)] if (kc == n_kc - 1 and b == 1) else (),
                          waits=fw if (kc == 0 and b == 0) else ())
                state["fm"] = tM.n
                if o2 == 1:
                    hs_free_t[og & 1] = tM.n
                evac(oc, pp, tM.n)

    for hf in range(2):
        DA(qXH, lambda s, hf=hf: s.dma_start(
            xh3[:, :, :],
            xn_w[:, :, HALO + hf * 1024: HALO + (hf + 1) * 1024]),
           waits=[(tM, tM.n)])
        xhq = qXH.n
        P.q["tensor"].append(lambda t, _v=xhq: t.wait_ge(qXH.h, _v))

        def ev_w1(oc, pp, fm):
            SE(lambda s, oc=oc, pp=pp: s.activation(
                h53[:, oc, :], pp[:, :], AF.Silu, scale=s_s1[:, oc:oc + 1]),
               waits=[(tM, fm)] + ([(qC2, qc2_q)] if oc == 0 else []))
            p5_free_s[oc & 3] = sC.n

        def ev_w2(oc, pp, fm):
            V(lambda v, oc=oc, pp=pp: v.scalar_tensor_tensor(
                h53[:, oc, :], pp[:, :], s_s2[:, oc:oc + 1], h53[:, oc, :],
                ALU.mult, ALU.mult),
              waits=[(tM, fm), (sC, sC.n)])
            p5_free_v[oc & 3] = vC.n

        gemm_slab(w1_v, 8, xh3, 32, ev_w1)
        gemm_slab(w2_v, 8, xh3, 32, ev_w2)

        def mk_ev_h(dst3, ssc):
            def ev(oc, pp, fm):
                SE(lambda s, oc=oc, pp=pp: s.activation(
                    dst3[:, oc, :], pp[:, :], AF.Gelu, scale=ssc[:, oc:oc + 1]),
                   waits=[(tM, fm)])
                p5_free_s[oc & 3] = sC.n
            return ev

        gemm_slab(h1T.ap().rearrange("(k p) o -> p k o", p=128), 32, h53, 32,
                  mk_ev_h(hb53, s_sh1), gate=[(vC, vC.n), (sC, sC.n)])
        gemm_slab(h2T.ap().rearrange("(k p) o -> p k o", p=128), 32, hb53, 32,
                  mk_ev_h(h53, s_sh2), gate=[(sC, sC.n)])

        def mk_ev_w3(hf):
            def ev(oc, pp, fm):
                DA(qRX, lambda s, oc=oc, hf=hf: s.dma_start(
                    s_rr[:, :], d_r2.ap()[oc][:, hf * 1024:(hf + 1) * 1024]),
                   waits=[(vC, vC.n)])
                DA(qRX, lambda s, oc=oc, hf=hf: s.dma_start(
                    s_xx[:, :], xTv2[:, oc, HALO + hf * 1024: HALO + (hf + 1) * 1024]),
                   waits=[(vC, vC.n)])
                xq5 = qRX.n
                V(lambda v, oc=oc, pp=pp: v.scalar_tensor_tensor(
                    s_oo[:, :], pp[:, :], s_s3[:, oc:oc + 1], s_rr[:, :], ALU.mult, ALU.add),
                  waits=[(tM, fm), (qRX, xq5), (qOUT, qOUT.n)])
                V(lambda v: v.tensor_add(s_oo[:, :], s_oo[:, :], s_xx[:, :]))
                p5_free_v[oc & 3] = vC.n
                vv = vC.n
                DA(qOUT, lambda s, oc=oc, hf=hf: s.dma_start(
                    out_v[:, oc, hf * 1024:(hf + 1) * 1024], s_oo[:, :]),
                   waits=[(vC, vv)])
            return ev

        gemm_slab(w3_v, 32, h53, 8, mk_ev_w3(hf), gate=[(sC, sC.n)])

    _finish(nc, P, _dma_sems)
    st5.close()
    ctx.close()
    return nc


_NC_CACHE = {}


def kernel(**inputs):
    x = np.asarray(inputs["L0_attn_x"], dtype=np.float32)
    offsets = np.asarray(inputs["L0_attn_offsets"]).astype(np.int64)
    offs = sorted(int(o) for o in offsets.tolist())
    OMAX = max(offs)
    HALO = ((OMAX + 127) // 128) * 128

    key = (tuple(offs),)
    if key not in _NC_CACHE:
        _NC_CACHE[key] = build_nc(offsets)
    nc = _NC_CACHE[key]

    Bsz, N, Dm = x.shape
    per_b = NCORES // Bsz
    assert N // per_b == OWN and Dm == D

    def cvt(a):
        return np.ascontiguousarray(np.asarray(a, dtype=np.float32))

    shared = {
        "wqkv": cvt(inputs["L0_attn_w_qkv"]),
        "wgate": cvt(inputs["L0_attn_w_gate"]),
        "wout": cvt(inputs["L0_attn_w_out"]),
        "wfg": cvt(inputs["L0_attn_w_ffn_gate"]),
        "wfu": cvt(inputs["L0_attn_w_ffn_up"]),
        "wfd": cvt(inputs["L0_attn_w_ffn_down"]),
        "w1T": cvt(np.asarray(inputs["L0_ffn_w1_w"], dtype=np.float32).T),
        "w2T": cvt(np.asarray(inputs["L0_ffn_w2_w"], dtype=np.float32).T),
        "h1T": cvt(np.asarray(inputs["L0_ffn_h1_w"], dtype=np.float32).T),
        "h2T": cvt(np.asarray(inputs["L0_ffn_h2_w"], dtype=np.float32).T),
        "w3T": cvt(np.asarray(inputs["L0_ffn_w3_w"], dtype=np.float32).T),
        "g1v": cvt(inputs["L0_attn_norm1_scale"]),
        "g2v": cvt(inputs["L0_attn_norm2_scale"]),
        "bgv": cvt(inputs["L0_attn_b_gate"]),
        "s1v": cvt(inputs["L0_ffn_w1_s"]),
        "s2v": cvt(inputs["L0_ffn_w2_s"]),
        "sh1v": cvt(inputs["L0_ffn_h1_s"]),
        "sh2v": cvt(inputs["L0_ffn_h2_s"]),
        "s3v": cvt(inputs["L0_ffn_w3_s"]),
        "posb": cvt(inputs["L0_attn_pos_bias"]),
        "c_iota": np.arange(128, dtype=np.float32).reshape(128, 1),
        "c_onesf": np.ones((1, 128), dtype=np.float32),
        "c_onesb": np.ones((128, 128), dtype=BF),
        "c_onescb": np.ones((128, 1), dtype=BF),
        "c_id": np.eye(128, dtype=BF),
        "c_cv": np.array([[1.0e-5, 1.0e-6, -0.5, 0.0]], dtype=np.float32),
    }

    in_maps = []
    for c in range(NCORES):
        b = c // per_b
        s0 = (c % per_b) * OWN
        lo = s0 - HALO
        if lo >= 0:
            seg = x[b, lo:s0 + OWN, :]
        else:
            seg = np.concatenate(
                [np.zeros((-lo, D), dtype=np.float32), x[b, 0:s0 + OWN, :]], axis=0)
        m = dict(shared)
        m["xT"] = np.ascontiguousarray(seg.T)
        m["tbase"] = np.full((128, 1), float(s0), dtype=np.float32)
        in_maps.append(m)

    res = run_bass_kernel_spmd(nc, in_maps, core_ids=list(range(NCORES)))

    out = np.empty((Bsz, N, D), dtype=np.float32)
    for c in range(NCORES):
        b = c // per_b
        s0 = (c % per_b) * OWN
        out[b, s0:s0 + OWN, :] = np.asarray(res.results[c]["out"]).T
    return out
